# version 15
# baseline (speedup 1.0000x reference)
import math
import numpy as np
import ml_dtypes
from contextlib import ExitStack
import concourse.bass as bass
import concourse.mybir as mybir
from concourse.bass_utils import run_bass_kernel_spmd

F32 = mybir.dt.float32
BF16 = mybir.dt.bfloat16
I8 = mybir.dt.int8
AF = mybir.ActivationFunctionType
ALU = mybir.AluOpType
AX = mybir.AxisListType

HID = 4096
DKV = 1024
DQ = 4096
D = 128
KVH = 8
HQ = 32
T = 1024
HLO = 64
TH = T + HLO
NCH = 8
W = 64
MAGIC = 12582912.0
RSQD = 1.0 / math.sqrt(128.0)
G16 = 1.0 / 16.0
EPS = 1e-5
NCORES = 8
BF = ml_dtypes.bfloat16


class Trk:
    def __init__(self, nc, es):
        self.nc = nc
        self.es = es
        self.sems = {}
        self.waited = {}
        self.w = {}
        self.r = {}

    def sem(self, name):
        if name not in self.sems:
            self.sems[name] = [self.es.enter_context(self.nc.semaphore(name)), 0]
        return self.sems[name]

    def _wait(self, eng, engname, semname, val):
        wt = self.waited.setdefault(engname, {})
        if wt.get(semname, 0) < val:
            eng.wait_ge(self.sems[semname][0], val)
            wt[semname] = val

    def dep(self, eng, engname, reads=(), writes=()):
        need = {}
        for k in reads:
            lw = self.w.get(k)
            if lw is not None:
                need[lw[0]] = max(need.get(lw[0], 0), lw[1])
        for k in writes:
            lw = self.w.get(k)
            if lw is not None:
                need[lw[0]] = max(need.get(lw[0], 0), lw[1])
            for (sn, sv) in self.r.get(k, ()):
                need[sn] = max(need.get(sn, 0), sv)
        for sn, sv in need.items():
            if sn.startswith("d"):
                sv = self.sems[sn][1]
            self._wait(eng, engname, sn, sv)

    def done(self, inst, semname, inc, reads=(), writes=()):
        s = self.sem(semname)
        s[1] += inc
        inst.then_inc(s[0], inc)
        for k in writes:
            self.w[k] = (semname, s[1])
            self.r[k] = []
        for k in reads:
            self.r.setdefault(k, []).append((semname, s[1]))


class KB:

    def __init__(self, nc, stop_after=None):
        self.nc = nc
        self.stop_after = stop_after
        self.es = ExitStack()
        self.trk = Trk(nc, self.es)
        self.scopes = []

    def sb(self, name, p, f, dt, scoped=True, scope_idx=None):
        cm = self.nc.sbuf_tensor(name, [p, f], dt)
        if scope_idx is not None and self.scopes:
            return self.scopes[scope_idx].enter_context(cm)
        if scoped and self.scopes:
            return self.scopes[-1].enter_context(cm)
        return self.es.enter_context(cm)

    def push_scope(self):
        self.scopes.append(ExitStack())

    def pop_scope(self):
        self.scopes.pop().close()
        self.barrier()

    def dma(self, q, qname, semname, outap, inap, reads=(), writes=(), transpose=False, accum=False):
        self.trk.dep(q, qname, reads=reads, writes=writes)
        if transpose:
            i = q.dma_start(out=outap, in_=inap, transpose=True)
        elif accum:
            i = q.dma_start(out=outap, in_=inap, accum_op=ALU.add)
        else:
            i = q.dma_start(out=outap, in_=inap)
        self.trk.done(i, semname, 16, reads=reads, writes=writes)

    def op(self, eng, engname, semname, fn, reads=(), writes=()):
        self.trk.dep(eng, engname, reads=reads, writes=writes)
        i = fn(eng)
        self.trk.done(i, semname, 1, reads=reads, writes=writes)

    def vop(self, fn, reads=(), writes=()):
        self.op(self.nv, "v", "tv", fn, reads, writes)

    def sop(self, fn, reads=(), writes=()):
        self.op(self.ns, "s", "ts", fn, reads, writes)

    def top(self, fn, reads=(), writes=()):
        self.op(self.nt, "t", "tt", fn, reads, writes)

    def barrier(self):
        t = self.trk
        for sn in list(t.sems):
            if t.sems[sn][1] > 0 and not sn.startswith("bar"):
                t._wait(self.ny, "y", sn, t.sems[sn][1])
        name = f"bar{len(t.sems)}"
        s = t.sem(name)
        self.ny.nop().then_inc(s[0], 1)
        s[1] += 1
        engs = [(self.nv, "v"), (self.ns, "s"), (self.nt, "t"), (self.ng, "g"), (self.ny, "y")]
        for eng, en in engs:
            t._wait(eng, en, name, 1)
        for en in list(t.waited):
            t.waited[en] = {name: 1}
        t.w = {k: (name, 1) for k in t.w}
        t.r = {}

    def build(self):
        nc = self.nc
        din = lambda name, shape, dt: nc.dram_tensor(name, shape, dt, kind="ExternalInput").ap()
        self.x_in = din("x", [TH, HID], F32)
        self.xh_in = din("xh", [T, HID], F32)
        self.wk_in = din("wk", [KVH, 128, 32, 128], F32)
        self.wv_in = din("wv", [2, 128, 32, 512], F32)
        self.wq_in = din("wq", [HQ, 128, 32, 128], F32)
        self.wo_in = din("wo", [8, 128, 32, 512], F32)
        self.wf_in = din("wf", [8, 128, 32, 512], I8)
        self.wsb_in = din("wsb", [128, HID], F32)
        self.bb_in = din("bb", [128, HID], F32)
        self.cosf_in = din("cosf", [128, TH], F32)
        self.sinf_in = din("sinf", [128, TH], F32)
        self.mask0_in = din("mask0", [128, W], F32)
        self.maskg_in = din("maskg", [128, W], F32)
        self.gmask_in = din("gmask", [128, 128], F32)
        self.ident_in = din("ident", [128, 128], BF16)
        self.out = nc.dram_tensor("out", [T, HID], F32, kind="ExternalOutput").ap()

        dsc = lambda name, shape, dt: nc.dram_tensor(name, shape, dt, kind="Internal").ap()
        self.ln_scr = dsc("ln_scr", [TH, HID], BF16)
        self.lnh_scr = dsc("lnh_scr", [T, HID], BF16)
        self.v_scr = dsc("v_scr", [TH, DKV], BF16)
        self.a_scr = dsc("a_scr", [KVH * 128, T], F32)
        self.xq_scr = dsc("xq_scr", [T, HID], BF16)
        self.attn_scr = dsc("attn_scr", [T, DQ], BF16)
        self.kt_scr = dsc("kt_scr", [KVH, 128, T], BF16)
        self.kh_scr = dsc("kh_scr", [KVH, 128, T], BF16)
        self.khh_scr = dsc("khh_scr", [KVH, 128, T], BF16)
        self.sk_scr = dsc("sk_scr", [KVH, 128, TH], BF16)

        self.cpool = self.sb("cpool", 128, 16, F32, scoped=False)
        self.onescol = self.sb("onescol", 128, 1, BF16, scoped=False)
        self.masks = self.sb("s_masks", 128, 2 * W, F32, scoped=False)
        self.mask0 = self.masks[:, 0:W]
        self.maskg = self.masks[:, W:2 * W]
        self.gmask = self.sb("s_gmask", 128, 128, F32, scoped=False)
        self.ident = self.sb("s_ident", 128, 128, BF16, scoped=False)
        self.small = self.sb("small", 128, 160, F32, scoped=False)
        self.stat = self.small[:, 0:32]
        self.colb = self.small[:, 32:40]
        self.aend = self.small[:, 40:48]
        self.rzk = self.small[:, 48:112]
        self.xsc = self.small[:, 112:120]
        self.sswa = self.small[:, 120:152]
        self.s32 = self.sb("s32", 128, KVH * 128, F32, scoped=False)
        self.sbf = self.sb("sbf", 128, KVH * 128, BF16, scoped=False)

        self.pb = [self.es.enter_context(nc.psum_tensor(f"pb{i}", [128, 512], F32)) for i in range(7)]
        self.pbt = self.es.enter_context(nc.psum_tensor("pbt", [128, 1024], BF16))

        blk = self.es.enter_context(nc.Block())

        @blk.sync
        def _(root):
            self.nv, self.ns, self.nt, self.ng, self.ny = nc.vector, nc.scalar, nc.tensor, nc.gpsimd, root
            self.emit()

        self.es.close()
        return nc

    def emit(self):
        vop, sop, top, dma = self.vop, self.sop, self.top, self.dma
        nv, ns, nt, ng, ny = self.nv, self.ns, self.nt, self.ng, self.ny
        pb, pbt = self.pb, self.pbt

        vop(lambda e: e.memset(self.cpool[:, :], 0.0), writes=["cpool"])
        vop(lambda e: e.memset(self.cpool[:, 8:9], 1.0), writes=["ones32", "cpool"])
        vop(lambda e: e.memset(self.cpool[:, 9:10], EPS), writes=["cEPS", "cpool"])
        vop(lambda e: e.memset(self.onescol[:, :], 1.0), writes=["onesb"])
        dma(ng, "g", "dck", self.mask0[:, :], self.mask0_in, writes=["mask0"])
        dma(ng, "g", "dck", self.maskg[:, :], self.maskg_in, writes=["maskg"])
        dma(ng, "g", "dck", self.gmask[:, :], self.gmask_in, writes=["gmask"])
        dma(ng, "g", "dck", self.ident[:, :], self.ident_in, writes=["ident"])


        self.push_scope()
        self.lnq = self.sb("lnq", 128, 32 * TH, BF16)
        self.wkb = self.sb("wkb", 128, 2 * 32 * 128, BF16)
        self.cosf = self.sb("s_cosf", 128, TH, F32)
        self.sinf = self.sb("s_sinf", 128, TH, F32)
        dma(self.ng, "g", "dck", self.cosf[:, :], self.cosf_in, writes=["cosf"])
        dma(self.ng, "g", "dck", self.sinf[:, :], self.sinf_in, writes=["sinf"])
        sc1 = self.sb("bc1", 128, TH, F32)
        sc2 = self.sb("bc2", 128, TH, F32)
        sc3 = self.sb("bc3", 128, T, F32)
        sc4 = self.sb("bc4", 128, T, F32)
        self.bscr = (sc1, sc2, sc3, sc4)

        self.push_scope()
        scr1 = self.sb("pa1", 128, 2 * HID, F32)
        scr2d = self.sb("pa2", 128, HID, F32)
        scrb = self.sb("pab", 128, 2 * HID, BF16)
        self.emit_ln(self.xh_in, T, self.lnh_scr, "H", scr1, scr2d, scrb)
        self.emit_ln(self.x_in, TH, self.ln_scr, "L", scr1, scr2d, scrb)
        self.pop_scope()
        if self.stop_after == "A":
            self.pop_scope()
            return

        vop(lambda e: e.memset(self.s32[:, :], 0.0), writes=["s32"])
        self.push_scope()
        self.wvb = self.sb("wvb", 128, 32 * 512, BF16)
        self.kfm = self.sb("kfm", 128, 4 * TH, F32)
        self.vtok = self.sb("vtok", 128, 9 * DKV, BF16)
        self.stg = self.sb("stg", 128, T, BF16)
        self.stg2 = self.sb("stg2", 128, TH, BF16)
        self.scb = self.sb("bcb", 128, 256, BF16)
        self.khc2 = self.sb("khc2", 128, KVH * 128, BF16)

        self.emit_kvphase(self.lnh_scr, T, "H", halo=True)
        self.load_lnq(self.ln_scr, TH, "L")
        self.emit_halostate()
        for kvh in range(KVH):
            sop(lambda e, kvh=kvh: e.activation(out=self.stat[:, 12:13], in_=self.aend[:, kvh:kvh + 1],
                                                func=AF.Exp, scale=G16),
                reads=["aend"], writes=["cbE"])
            sop(lambda e, kvh=kvh: e.activation(out=self.s32[:, kvh * 128:(kvh + 1) * 128],
                                                in_=self.s32[:, kvh * 128:(kvh + 1) * 128],
                                                func=AF.Copy, scale=self.stat[:, 12:13]),
                reads=["cbE", "s32"], writes=["s32"])
        sop(lambda e: e.activation(out=self.sbf[:, :], in_=self.s32[:, :], func=AF.Copy),
            reads=["s32"], writes=["sbf"])
        if self.stop_after == "B":
            self.pop_scope()
            self.pop_scope()
            return
        self.emit_kvphase(self.ln_scr, TH, "L", halo=False, skip_lnq=True)
        self.pop_scope()
        if self.stop_after == "C":
            self.pop_scope()
            return

        self.push_scope()
        self.qha = self.sb("qha", 128, 16 * T, BF16)
        self.sqh = self.sb("sqh", 128, 16 * T, BF16)
        self.svw = self.sb("svw", 128, 2 * 8 * 128, BF16)
        self.sct = self.sb("sct", 128, 4 * 512, BF16)
        self.ptb = self.sb("ptb", 128, 4 * 512, BF16)
        self.omix = self.sb("omix", 128, 2 * 512, BF16)
        self.ahq = self.sb("ahq", 128, 512, F32)
        self.scal = self.sb("scal", 128, 256, F32)
        self.qrw = self.sb("qrw", 128, 512, F32)
        self.ktc = self.sb("ktc", 128, 2 * 4 * 128, BF16)
        self.khc = self.sb("khc", 128, 4 * 128, BF16)
        self.skw = self.sb("skw", 128, 2 * 4 * 192, BF16)
        for hh in range(2):
            self.emit_qheads(hh)
            if self.stop_after and self.stop_after.startswith("Q0"):
                break
            self.emit_chunk_loads(0, hh)
            for C in range(NCH):
                if C + 1 < NCH:
                    self.emit_chunk_loads(C + 1, hh)
                self.emit_chunk(C, hh)
                if self.stop_after in ("K0", "K1") and C >= (0 if self.stop_after == "K0" else 1):
                    break
            if self.stop_after in ("K0", "K1"):
                break
        self.pop_scope()
        self.pop_scope()
        if self.stop_after == "D":
            return

        self.push_scope()
        afm = self.sb("afm", 128, 32 * T, BF16)
        wob = self.sb("wob", 128, 2 * 32 * 512, BF16)
        xrow = self.sb("xrow", 128, 2 * 512, F32)
        x2p = self.sb("x2p", 128, 2 * 512, F32)
        x2sb = self.sb("x2sb", 128, HID, F32)
        qf1 = self.sb("qf1", 128, HID, F32)
        qf2 = self.sb("qf2", 128, HID, F32)
        qfb = self.sb("qfb", 128, HID, BF16)
        for k in range(32):
            dma(ny, "y", "dtr", afm[:, T * k:T * k + T],
                self.attn_scr[0:T, 128 * k:128 * (k + 1)],
                reads=["ascr2"], writes=["afm"], transpose=True)
        for hs in range(8):
            ws = hs % 2
            dma(ng, "g", f"dwo{ws}",
                wob.ap().rearrange("p (s k n) -> p s k n", s=2, k=32)[:, ws],
                self.wo_in[hs], writes=[f"wob{ws}"])
            for tc in range(8):
                bank = tc % 2
                pk = f"pb{bank}"
                sl = tc % 2
                dma(ny, "y", f"dxr{sl}", xrow[:, 512 * sl:512 * (sl + 1)],
                    self.x_in[HLO + 128 * tc:HLO + 128 * (tc + 1), 512 * hs:512 * (hs + 1)],
                    writes=[f"xrow{sl}"])
                for kk in range(32):
                    top(lambda e, kk=kk, tc=tc, bank=bank, ws=ws:
                        e.matmul(out=pb[bank][:, 0:512],
                                 lhsT=afm[:, T * kk + 128 * tc:T * kk + 128 * (tc + 1)],
                                 rhs=wob[:, (ws * 32 + kk) * 512:(ws * 32 + kk + 1) * 512],
                                 start=(kk == 0), stop=(kk == 31)),
                        reads=["afm", f"wob{ws}"], writes=[pk])
                vop(lambda e, bank=bank, sl=sl: e.tensor_add(out=x2p[:, 512 * sl:512 * (sl + 1)],
                                                             in0=pb[bank][:, 0:512],
                                                             in1=xrow[:, 512 * sl:512 * (sl + 1)]),
                    reads=[pk, f"xrow{sl}"], writes=[f"x2p{sl}"])
                dma(ny, "y", "dx2w", self.out[128 * tc:128 * (tc + 1), 512 * hs:512 * (hs + 1)],
                    x2p[:, 512 * sl:512 * (sl + 1)], reads=[f"x2p{sl}"], writes=["x2scr"])
                if hs == 7:
                    self.emit_quant(tc, x2sb, qf1, qf2, qfb)
        self.pop_scope()
        if self.stop_after == "E":
            return

        self.push_scope()
        xqf = self.sb("xqf", 128, 32 * T, BF16)
        wfb = self.sb("wfb", 128, 2 * 32 * 512, BF16)
        wsb = self.sb("s_wsb", 128, HID, F32)
        bbs = self.sb("bbs", 128, HID, F32)
        g1 = self.sb("g1", 128, 512, F32)
        g2 = self.sb("g2", 128, 512, F32)
        g3 = self.sb("g3", 128, 512, F32)
        dma(ng, "g", "dck", wsb[:, :], self.wsb_in, writes=["wsb"])
        dma(ng, "g", "dck", bbs[:, :], self.bb_in, writes=["bb"])
        for k in range(32):
            dma(ny, "y", "dtr2", xqf[:, T * k:T * k + T],
                self.xq_scr[0:T, 128 * k:128 * (k + 1)],
                reads=["xqscr"], writes=["xqf"], transpose=True)
        nst = 0
        for os_ in range(8):
            ws = os_ % 2
            dma(ng, "g", f"dwf{ws}",
                wfb.ap().rearrange("p (s k n) -> p s k n", s=2, k=32)[:, ws],
                self.wf_in[os_], writes=[f"wfb{ws}"])
            for tc in range(8):
                bank = tc % 2
                pk = f"pb{bank}"
                for kk in range(32):
                    top(lambda e, kk=kk, tc=tc, bank=bank, ws=ws:
                        e.matmul(out=pb[bank][:, 0:512],
                                 lhsT=xqf[:, T * kk + 128 * tc:T * kk + 128 * (tc + 1)],
                                 rhs=wfb[:, (ws * 32 + kk) * 512:(ws * 32 + kk + 1) * 512],
                                 start=(kk == 0), stop=(kk == 31)),
                        reads=["xqf", f"wfb{ws}"], writes=[pk])
                sop(lambda e, tc=tc, bank=bank: e.activation(out=g1[:, :], in_=pb[bank][:, 0:512],
                                                             func=AF.Copy,
                                                             scale=self.xsc[:, tc:tc + 1]),
                    reads=[pk, "xsc"], writes=["g1"])
                vop(lambda e, os_=os_: e.tensor_mul(out=g2[:, :], in0=g1[:, :],
                                                    in1=wsb[:, 512 * os_:512 * (os_ + 1)]),
                    reads=["g1", "wsb"], writes=["g2"])
                vop(lambda e, os_=os_: e.tensor_tensor(out=g2[:, :], in0=g2[:, :],
                                                       in1=bbs[:, 512 * os_:512 * (os_ + 1)],
                                                       op=ALU.add),
                    reads=["g2", "bb"], writes=["g2"])
                dma(ng, "g", "dout", self.out[128 * tc:128 * (tc + 1), 512 * os_:512 * (os_ + 1)],
                    g2[:, :], reads=["g2"], writes=["outw", "x2scr"], accum=True)
                nst += 1
        self.ng.wait_ge(self.trk.sems["dout"][0], 16 * nst)
        self.ny.wait_ge(self.trk.sems["dout"][0], 16 * nst)
        self.pop_scope()


    def emit_quant(self, tc, x2sb, qf1, qf2, qfb):
        vop, sop, dma = self.vop, self.sop, self.dma
        st = self.stat
        dma(self.ny, "y", "dx2r", x2sb[:, :], self.out[128 * tc:128 * (tc + 1), :],
            reads=["x2scr"], writes=["x2sb"])
        sop(lambda e: e.activation(out=qf1[:, :], in_=x2sb[:, :], func=AF.Square,
                                   accum_out=st[:, 0:1]),
            reads=["x2sb"], writes=["f0", "fs0"])
        vop(lambda e: e.reduce_sum(out=st[:, 1:2], in_=x2sb[:, :], axis=AX.X),
            reads=["x2sb"], writes=["fs1"])
        vop(lambda e: e.tensor_scalar(out=st[:, 2:3], in0=st[:, 1:2], scalar1=1.0 / HID,
                                      scalar2=None, op0=ALU.mult),
            reads=["fs1"], writes=["fs2"])
        vop(lambda e: e.tensor_scalar(out=st[:, 3:4], in0=st[:, 2:3], scalar1=st[:, 2:3],
                                      scalar2=None, op0=ALU.mult),
            reads=["fs2"], writes=["fs3"])
        vop(lambda e: e.scalar_tensor_tensor(out=st[:, 4:5], in0=st[:, 0:1], scalar=1.0 / HID,
                                             in1=st[:, 3:4], op0=ALU.mult, op1=ALU.subtract),
            reads=["fs0", "fs3"], writes=["fs4"])
        sop(lambda e: e.activation(out=st[:, 5:6], in_=st[:, 4:5], func=AF.Sqrt,
                                   bias=self.cpool[:, 9:10]),
            reads=["fs4", "cEPS"], writes=["fs5"])
        vop(lambda e: e.reciprocal(out=st[:, 6:7], in_=st[:, 5:6]),
            reads=["fs5"], writes=["fs6"])
        vop(lambda e: e.tensor_scalar(out=qf1[:, :], in0=x2sb[:, :],
                                      scalar1=st[:, 2:3], scalar2=st[:, 6:7],
                                      op0=ALU.subtract, op1=ALU.mult),
            reads=["x2sb", "fs2", "fs6"], writes=["f0"])
        vop(lambda e: e.reduce_max(out=st[:, 8:9], in_=qf1[:, :], axis=AX.X,
                                   apply_absolute_value=True),
            reads=["f0"], writes=["fs8"])
        vop(lambda e, tc=tc: e.tensor_scalar(out=self.xsc[:, tc:tc + 1], in0=st[:, 8:9],
                                             scalar1=1e-8, scalar2=1.0 / 127.0,
                                             op0=ALU.max, op1=ALU.mult),
            reads=["fs8"], writes=["xsc"])
        vop(lambda e, tc=tc: e.reciprocal(out=st[:, 9:10], in_=self.xsc[:, tc:tc + 1]),
            reads=["xsc"], writes=["fs9"])
        sop(lambda e: e.activation(out=qf2[:, :], in_=qf1[:, :], func=AF.Copy, scale=st[:, 9:10]),
            reads=["f0", "fs9"], writes=["f2"])
        vop(lambda e: e.tensor_scalar(out=qf1[:, :], in0=qf2[:, :], scalar1=MAGIC, scalar2=None,
                                      op0=ALU.add),
            reads=["f2"], writes=["f0"])
        vop(lambda e: e.tensor_scalar(out=qf2[:, :], in0=qf1[:, :], scalar1=MAGIC, scalar2=127.0,
                                      op0=ALU.subtract, op1=ALU.min),
            reads=["f0"], writes=["f2"])
        vop(lambda e: e.tensor_scalar(out=qfb[:, :], in0=qf2[:, :], scalar1=-128.0, scalar2=None,
                                      op0=ALU.max),
            reads=["f2"], writes=["fb"])
        dma(self.ny, "y", "dxq", self.xq_scr[128 * tc:128 * (tc + 1), :], qfb[:, :],
            reads=["fb"], writes=["xqscr"])

    def emit_ln(self, src_ap, nrows, dst_scr, tag, scr1, scr2d, scrb):
        vop, sop, dma = self.vop, self.sop, self.dma
        st = self.stat
        r0, ti = 0, 0
        while r0 < nrows:
            rr = min(128, nrows - r0)
            sl = ti % 2
            xk = f"lnx{sl}"
            bk = f"lnb{sl}"
            x1 = scr1[:, HID * sl:HID * (sl + 1)]
            xb = scrb[:, HID * sl:HID * (sl + 1)]
            dma(self.ny, "y", "dlx", x1[0:rr, :], src_ap[r0:r0 + rr, :], writes=[xk])
            sop(lambda e, rr=rr, x1=x1: e.activation(out=scr2d[0:rr, :],
                                              in_=x1[0:rr, :], func=AF.Square,
                                              accum_out=st[0:rr, 0:1]),
                reads=[xk], writes=["lsq", "l0"])
            vop(lambda e, rr=rr, x1=x1: e.reduce_sum(out=st[0:rr, 1:2], in_=x1[0:rr, :], axis=AX.X),
                reads=[xk], writes=["l1"])
            vop(lambda e, rr=rr: e.tensor_scalar(out=st[0:rr, 2:3], in0=st[0:rr, 1:2],
                                                 scalar1=1.0 / HID, scalar2=None, op0=ALU.mult),
                reads=["l1"], writes=["l2"])
            vop(lambda e, rr=rr: e.tensor_scalar(out=st[0:rr, 3:4], in0=st[0:rr, 2:3],
                                                 scalar1=st[0:rr, 2:3], scalar2=None, op0=ALU.mult),
                reads=["l2"], writes=["l3"])
            vop(lambda e, rr=rr: e.scalar_tensor_tensor(out=st[0:rr, 4:5], in0=st[0:rr, 0:1],
                                                        scalar=1.0 / HID, in1=st[0:rr, 3:4],
                                                        op0=ALU.mult, op1=ALU.subtract),
                reads=["l0", "l3"], writes=["l4"])
            sop(lambda e, rr=rr: e.activation(out=st[0:rr, 5:6], in_=st[0:rr, 4:5], func=AF.Sqrt,
                                              bias=self.cpool[0:rr, 9:10]),
                reads=["l4", "cEPS"], writes=["l5"])
            vop(lambda e, rr=rr: e.reciprocal(out=st[0:rr, 6:7], in_=st[0:rr, 5:6]),
                reads=["l5"], writes=["l6"])
            vop(lambda e, rr=rr, x1=x1, xb=xb: e.tensor_scalar(out=xb[0:rr, :], in0=x1[0:rr, :],
                                                 scalar1=st[0:rr, 2:3], scalar2=st[0:rr, 6:7],
                                                 op0=ALU.subtract, op1=ALU.mult),
                reads=[xk, "l2", "l6"], writes=[bk])
            dma(self.ny, "y", "dlw", dst_scr[r0:r0 + rr, :], xb[0:rr, :],
                reads=[bk], writes=[f"{tag}scr"])
            r0 += rr
            ti += 1

    def load_lnq(self, scr_ap, ttot, tag):
        for k in range(32):
            q, qn, sem = ((self.ny, "y", "dtr") if k % 2 == 0 else (self.ns, "s", "dts"))
            self.dma(q, qn, sem, self.lnq[:, TH * k:TH * k + ttot],
                     scr_ap[0:ttot, 128 * k:128 * (k + 1)],
                     reads=[f"{tag}scr"], writes=["lnq"], transpose=True)

    def emit_kvphase(self, scr_ap, ttot, tag, halo, skip_lnq=False):
        vop, sop, top, dma = self.vop, self.sop, self.top, self.dma
        pb = self.pb
        if not skip_lnq:
            self.load_lnq(scr_ap, ttot, tag)
        tsl = [(0, 512), (512, 512), (1024, 64)] if ttot == TH else [(0, 512), (512, 512)]
        for half in range(2):
            vg = half
            dma(self.ng, "g", "dwv", self.wvb.ap().rearrange("p (k n) -> p k n", k=32),
                self.wv_in[vg], writes=["wvb"])
            if half == 0:
                for kvh in range(KVH):
                    ws_ = kvh % 2
                    dma(self.ng, "g", f"dwk{ws_}",
                        self.wkb.ap().rearrange("p (s k n) -> p s k n", s=2, k=32)[:, ws_],
                        self.wk_in[kvh], writes=[f"wkb{ws_}"])
                    kv4 = kvh % 4
                    for (t0, tn) in tsl:
                        bank = (kvh + t0 // 512) % 2
                        pk = f"pb{bank}"
                        for kk in range(32):
                            top(lambda e, kk=kk, t0=t0, tn=tn, bank=bank, ws_=ws_:
                                e.matmul(out=pb[bank][:, 0:tn],
                                         lhsT=self.wkb[:, (ws_ * 32 + kk) * 128:(ws_ * 32 + kk + 1) * 128],
                                         rhs=self.lnq[:, TH * kk + t0:TH * kk + t0 + tn],
                                         start=(kk == 0), stop=(kk == 31)),
                                reads=["lnq", f"wkb{ws_}"], writes=[pk])
                        sop(lambda e, kvh=kvh, t0=t0, tn=tn, bank=bank:
                            e.activation(out=self.kfm[:, (kvh % 4) * ttot + t0:(kvh % 4) * ttot + t0 + tn],
                                         in_=pb[bank][:, 0:tn], func=AF.Copy),
                            reads=[pk], writes=[f"kfm{kvh % 4}"])
                    if kvh == 3 or kvh == 7:
                        for kv4b in range(4):
                            self.emit_kchain(4 * (kvh // 4) + kv4b, kv4b, ttot, tag, halo)
            tc0 = 0
            while tc0 < ttot:
                tcn = min(128, ttot - tc0)
                if tag == "L":
                    if tc0 < HLO:
                        tcn = min(tcn, HLO - tc0)
                    else:
                        tcn = min(tcn, 128 - (tc0 - HLO) % 128)
                bank = 2 + (tc0 // 128) % 2
                pk = f"pb{bank}"
                for kk in range(32):
                    top(lambda e, kk=kk, tc0=tc0, tcn=tcn, bank=bank:
                        e.matmul(out=pb[bank][0:tcn, 0:512],
                                 lhsT=self.lnq[:, TH * kk + tc0:TH * kk + tc0 + tcn],
                                 rhs=self.wvb[:, 512 * kk:512 * (kk + 1)],
                                 start=(kk == 0), stop=(kk == 31)),
                        reads=["lnq", "wvb"], writes=[pk])
                if tag == "L":
                    vk = "vstg"
                    sop(lambda e, tcn=tcn, bank=bank:
                        e.activation(out=self.stg.ap().bitcast(BF16)[0:tcn, 0:512],
                                     in_=pb[bank][0:tcn, 0:512], func=AF.Copy),
                        reads=[pk], writes=[vk])
                    dma(self.ny, "y", "dvs",
                        self.v_scr[tc0:tc0 + tcn, 512 * vg:512 * (vg + 1)],
                        self.stg.ap().bitcast(BF16)[0:tcn, 0:512],
                        reads=[vk], writes=["vscr"])
                else:
                    tile, ro = 1 + tc0 // 128, tc0 % 128
                    vk = f"vt{tile}"
                    sop(lambda e, tile=tile, ro=ro, tcn=tcn, vg=vg, bank=bank:
                        e.activation(out=self.vtok.ap().rearrange("p (c n) -> p c n", c=9)
                                     [ro:ro + tcn, tile, 512 * vg:512 * (vg + 1)],
                                     in_=pb[bank][0:tcn, 0:512], func=AF.Copy),
                        reads=[pk], writes=[vk])
                tc0 += tcn


    def emit_kchain(self, kvh, kv4, ttot, tag, halo):
        vop, sop, top, dma = self.vop, self.sop, self.top, self.dma
        pb, pbt = self.pb, self.pbt
        sc1, sc2, sc3, sc4 = self.bscr
        scb = self.scb
        kk_ = f"kfm{kv4}"
        off = kv4 * ttot
        g0 = HLO if tag == "L" else 0
        kfm = self.kfm
        sop(lambda e: e.activation(out=sc1[:, 0:T], in_=kfm[:, off + g0:off + g0 + T], func=AF.Abs),
            reads=[kk_], writes=["c1"])
        sop(lambda e: e.activation(out=sc1[:, 0:T], in_=sc1[:, 0:T], func=AF.Exp, scale=-1.0),
            reads=["c1"], writes=["c1"])
        sop(lambda e: e.activation(out=sc1[:, 0:T], in_=sc1[:, 0:T], func=AF.Ln, bias=1.0),
            reads=["c1"], writes=["c1"])
        vop(lambda e: e.tensor_scalar_min(out=sc2[:, 0:T], in0=kfm[:, off + g0:off + g0 + T], scalar1=0.0),
            reads=[kk_], writes=["c2"])
        vop(lambda e: e.tensor_sub(out=sc2[:, 0:T], in0=sc2[:, 0:T], in1=sc1[:, 0:T]),
            reads=["c1", "c2"], writes=["c2"])
        vop(lambda e: e.tensor_tensor_scan(out=sc3[:, 0:T], data0=sc2[:, 0:T], data1=sc2[:, 0:T],
                                           initial=0.0, op0=ALU.add, op1=ALU.bypass),
            reads=["c2"], writes=["c3"])
        if halo:
            vop(lambda e, kvh=kvh: e.tensor_copy(out=self.aend[:, kvh:kvh + 1], in_=sc3[:, T - 1:T]),
                reads=["c3"], writes=["aend"])
        else:
            dma(self.ny, "y", "das", self.a_scr[128 * kvh:128 * (kvh + 1), :], sc3[:, 0:T],
                reads=["c3"], writes=["ascr"])
        vop(lambda e: e.scalar_tensor_tensor(out=sc4[:, 0:T], in0=sc3[:, 0:T], scalar=-G16,
                                             in1=kfm[:, off + g0:off + g0 + T],
                                             op0=ALU.mult, op1=ALU.add),
            reads=["c3", kk_], writes=["c4"])
        if not halo:
            sop(lambda e: e.activation(out=self.stg2[:, 0:T], in_=sc4[:, 0:T], func=AF.Exp),
                reads=["c4"], writes=["kstg"])
            dma(self.ny, "y", "dkt", self.kt_scr[kvh], self.stg2[:, 0:T],
                reads=["kstg"], writes=["ktscr"])
        sop(lambda e: e.activation(out=sc1[:, 0:T], in_=kfm[:, off + g0:off + g0 + T], func=AF.Exp),
            reads=[kk_], writes=["c1"])
        for c in range(NCH):
            top(lambda e, c=c: e.matmul(out=pb[4][:, c:c + 1],
                                        lhsT=sc1[:, 128 * c:128 * (c + 1)],
                                        rhs=self.cpool[:, 8:9], start=True, stop=True),
                reads=["c1", "ones32"], writes=["pb4"])
        vop(lambda e, kvh=kvh: e.reciprocal(out=self.rzk[:, kvh * NCH:(kvh + 1) * NCH],
                                            in_=pb[4][:, 0:NCH]),
            reads=["pb4"], writes=[f"rzk{kvh}"])
        vop(lambda e: e.tensor_scalar(out=self.colb[:, 0:8],
                                      in0=sc3[:, 0:T].rearrange("p (c n) -> p c n", c=8)[:, :, 127],
                                      scalar1=G16, scalar2=None, op0=ALU.mult),
            reads=["c3"], writes=["cb1"])
        for c in range(NCH):
            par = c % 2
            sop(lambda e, c=c, par=par: e.activation(out=scb[:, 128 * par:128 * (par + 1)],
                                            in_=sc4[:, 128 * c:128 * (c + 1)], func=AF.Exp,
                                            bias=self.colb[:, c:c + 1]),
                reads=["c4", "cb1"], writes=[f"c2b{par}"])
            top(lambda e, par=par: e.transpose(out=pbt[:, 128 * par:128 * (par + 1)],
                                               in_=scb[:, 128 * par:128 * (par + 1)],
                                               identity=self.ident[:, :]),
                reads=[f"c2b{par}", "ident"], writes=[f"pbtk{par}"])
            sop(lambda e, c=c, kvh=kvh, par=par:
                e.activation(out=self.stg2[:, 128 * c:128 * (c + 1)],
                             in_=pbt[:, 128 * par:128 * (par + 1)], func=AF.Copy,
                             scale=self.rzk[:, kvh * NCH + c:kvh * NCH + c + 1]),
                reads=[f"pbtk{par}", f"rzk{kvh}"], writes=["kstg"])
        if halo:
            dma(self.ny, "y", "dkh2", self.khh_scr[kvh], self.stg2[:, 0:T],
                reads=["kstg"], writes=["khhscr"])
        else:
            dma(self.ny, "y", "dkh", self.kh_scr[kvh], self.stg2[:, 0:T],
                reads=["kstg"], writes=["khscr"])
            vop(lambda e: e.tensor_mul(out=sc1[:, 0:ttot], in0=kfm[:, off:off + ttot],
                                       in1=self.cosf[:, 0:ttot]),
                reads=[kk_, "cosf"], writes=["c1"])
            vop(lambda e: e.tensor_scalar(out=sc2[0:64, 0:ttot], in0=kfm[64:128, off:off + ttot],
                                          scalar1=-1.0, scalar2=None, op0=ALU.mult),
                reads=[kk_], writes=["c2"])
            vop(lambda e: e.tensor_copy(out=sc2[64:128, 0:ttot], in_=kfm[0:64, off:off + ttot]),
                reads=[kk_], writes=["c2"])
            vop(lambda e: e.tensor_mul(out=sc2[:, 0:ttot], in0=sc2[:, 0:ttot], in1=self.sinf[:, 0:ttot]),
                reads=["c2", "sinf"], writes=["c2"])
            vop(lambda e, ttot=ttot:
                e.tensor_tensor(out=self.stg2[:, 0:ttot],
                                in0=sc1[:, 0:ttot], in1=sc2[:, 0:ttot], op=ALU.add),
                reads=["c1", "c2"], writes=["kstg"])
            dma(self.ny, "y", "dsk", self.sk_scr[kvh][:, 0:ttot], self.stg2[:, 0:ttot],
                reads=["kstg"], writes=["skscr"])

    def emit_halostate(self):
        vop, top, dma = self.vop, self.top, self.dma
        pb = self.pb
        for c in range(NCH):
            dma(self.ny, "y", "dkh3", self.khc2[:, :],
                self.khh_scr.rearrange("h p t -> p h t")[:, :, 128 * c:128 * (c + 1)],
                reads=["khhscr"], writes=["khc2"])
            for kvh in range(KVH):
                bank = 5 + kvh % 2
                pk = f"pb{bank}"
                top(lambda e, kvh=kvh, c=c, bank=bank:
                    e.matmul(out=pb[bank][:, 0:128],
                             lhsT=self.khc2[:, 128 * kvh:128 * (kvh + 1)],
                             rhs=self.vtok.ap().rearrange("p (t n) -> p t n", t=9)
                             [:, 1 + c, 128 * kvh:128 * (kvh + 1)],
                             start=True, stop=True),
                    reads=["khc2", f"vt{1 + c}"], writes=[pk])
                vop(lambda e, kvh=kvh, bank=bank: e.tensor_add(out=self.s32[:, 128 * kvh:128 * (kvh + 1)],
                                                    in0=self.s32[:, 128 * kvh:128 * (kvh + 1)],
                                                    in1=pb[bank][:, 0:128]),
                    reads=[pk, "s32"], writes=["s32"])

    def emit_qheads(self, hh):
        vop, sop, top, dma = self.vop, self.sop, self.top, self.dma
        pb = self.pb
        sc1, sc2, sc3, sc4 = self.bscr
        qrw = self.qrw
        for hl in range(16):
            head = 16 * hh + hl
            kvh = head // 4
            ws = head % 2
            dma(self.ng, "g", f"dwq{ws}",
                self.wkb.ap().rearrange("p (s k n) -> p s k n", s=2, k=32)[:, ws],
                self.wq_in[head], writes=[f"wkb{ws}"])
            for tp in range(2):
                t0 = 512 * tp
                dma(self.ny, "y", "dah", self.ahq[:, :],
                    self.a_scr[128 * kvh:128 * (kvh + 1), t0:t0 + 512],
                    reads=["ascr"], writes=["ahq"])
                bank = 2 * (hl % 2) + tp
                pk = f"pb{bank}"
                for kk in range(32):
                    top(lambda e, kk=kk, t0=t0, bank=bank, ws=ws:
                        e.matmul(out=pb[bank][:, 0:512],
                                 lhsT=self.wkb[:, (ws * 32 + kk) * 128:(ws * 32 + kk + 1) * 128],
                                 rhs=self.lnq[:, TH * kk + HLO + t0:TH * kk + HLO + t0 + 512],
                                 start=(kk == 0), stop=(kk == 31)),
                        reads=["lnq", f"wkb{ws}"], writes=[pk])
                sop(lambda e, bank=bank: e.activation(out=qrw[:, :], in_=pb[bank][:, 0:512],
                                                      func=AF.Copy),
                    reads=[pk], writes=["qrw"])
                vop(lambda e: e.scalar_tensor_tensor(out=sc1[:, 0:512],
                                                     in0=self.ahq[:, :],
                                                     scalar=G16, in1=qrw[:, :],
                                                     op0=ALU.mult, op1=ALU.add),
                    reads=["qrw", "ahq"], writes=["c1"])
                sop(lambda e, hl=hl, t0=t0:
                    e.activation(out=self.qha[:, T * hl + t0:T * hl + t0 + 512],
                                 in_=sc1[:, 0:512], func=AF.Exp),
                    reads=["c1"], writes=["qha"])
                sop(lambda e: e.activation(out=sc2[:, 0:512], in_=qrw[:, :], func=AF.Exp),
                    reads=["qrw"], writes=["c2"])
                for c4 in range(4):
                    top(lambda e, c4=c4, hl=hl, tp=tp:
                        e.matmul(out=pb[4][:, 8 * hl + 4 * tp + c4:8 * hl + 4 * tp + c4 + 1],
                                 lhsT=sc2[:, 128 * c4:128 * (c4 + 1)],
                                 rhs=self.cpool[:, 8:9], start=True, stop=True),
                        reads=["c2", "ones32"], writes=["pb4"])
                vop(lambda e, t0=t0: e.tensor_tensor(out=sc3[:, 0:512], in0=qrw[:, :],
                                                     in1=self.cosf[:, HLO + t0:HLO + t0 + 512],
                                                     op=ALU.mult),
                    reads=["qrw", "cosf"], writes=["c3"])
                vop(lambda e: e.tensor_scalar(out=sc4[0:64, 0:512], in0=qrw[64:128, :],
                                              scalar1=-1.0, scalar2=None, op0=ALU.mult),
                    reads=["qrw"], writes=["c4"])
                vop(lambda e: e.tensor_copy(out=sc4[64:128, 0:512], in_=qrw[0:64, :]),
                    reads=["qrw"], writes=["c4"])
                vop(lambda e, t0=t0: e.tensor_mul(out=sc4[:, 0:512], in0=sc4[:, 0:512],
                                                  in1=self.sinf[:, HLO + t0:HLO + t0 + 512]),
                    reads=["c4", "sinf"], writes=["c4"])
                vop(lambda e, hl=hl, t0=t0:
                    e.tensor_tensor(out=self.sqh[:, T * hl + t0:T * hl + t0 + 512],
                                    in0=sc3[:, 0:512], in1=sc4[:, 0:512], op=ALU.add),
                    reads=["c3", "c4"], writes=["sqh"])
        vop(lambda e: e.reciprocal(out=self.scal[:, 0:128], in_=pb[4][:, 0:128]),
            reads=["pb4"], writes=["rzq"])
        vop(lambda e: e.tensor_scalar(out=self.scal[:, 128:256], in0=self.scal[:, 0:128],
                                      scalar1=0.5, scalar2=None, op0=ALU.mult),
            reads=["rzq"], writes=["sgla"])

    def emit_chunk_loads(self, C, hh):
        dma = self.dma
        slot = C % 2
        dma(self.ny, "y", f"dktc{slot}", self.ktc[:, 512 * slot:512 * (slot + 1)],
            self.kt_scr.rearrange("h p t -> p h t")[:, 4 * hh:4 * hh + 4, 128 * C:128 * (C + 1)],
            reads=["ktscr"], writes=[f"ktc{slot}"])
        for jj in range(2):
            j = 2 * C + jj
            for kv4 in range(4):
                dma(self.ns, "s", f"dsv{slot}",
                    self.svw[:, 1024 * slot + 128 * (4 * jj + kv4):1024 * slot + 128 * (4 * jj + kv4 + 1)],
                    self.v_scr[64 * j:64 * j + 128, 128 * (4 * hh + kv4):128 * (4 * hh + kv4 + 1)],
                    reads=["vscr"], writes=[f"svw{slot}"])
        for kv4 in range(4):
            dma(self.ns, "s", f"dskw{slot}",
                self.skw[:, 768 * slot + 192 * kv4:768 * slot + 192 * (kv4 + 1)],
                self.sk_scr[4 * hh + kv4][:, 128 * C:128 * C + 192],
                reads=["skscr"], writes=[f"skw{slot}"])

    def emit_chunk(self, C, hh):
        vop, sop, top, dma = self.vop, self.sop, self.top, self.dma
        pb, pbt = self.pb, self.pbt
        slot = C % 2
        ktk, khk, svk = f"ktc{slot}", "khc", f"svw{slot}"
        vtk = svk
        kto, svo = 512 * slot, 1024 * slot
        dma(self.ny, "y", "dkhc", self.khc[:, :],
            self.kh_scr.rearrange("h p t -> p h t")[:, 4 * hh:4 * hh + 4, 128 * C:128 * (C + 1)],
            reads=["khscr"], writes=["khc"])
        for kv4 in range(4):
            kvh = 4 * hh + kv4
            scb_ = kv4 % 2
            psk = f"pb{scb_}"
            top(lambda e, kv4=kv4: e.matmul(out=pb[scb_][:, 0:512],
                                   lhsT=self.ktc[:, kto + 128 * kv4:kto + 128 * (kv4 + 1)],
                                   rhs=self.qha.ap().rearrange("p (h n) -> p h n", h=16)
                                   [:, 4 * kv4:4 * (kv4 + 1), 128 * C:128 * (C + 1)],
                                   start=True, stop=True),
                reads=[ktk, "qha"], writes=[psk])
            vop(lambda e, kv4=kv4, kvh=kvh: e.scalar_tensor_tensor(
                    out=self.sct.ap().rearrange("p (v h n) -> p v h n", v=4, h=4)[:, kv4],
                    in0=pb[scb_].ap().rearrange("p (h n) -> p h n", h=4),
                    scalar=self.rzk[:, kvh * NCH + C:kvh * NCH + C + 1],
                    in1=self.gmask[:, :].unsqueeze(1).broadcast_to((128, 4, 128)),
                    op0=ALU.mult, op1=ALU.mult),
                reads=[psk, f"rzk{kvh}", "gmask"], writes=["sct"])
            for jj in range(2):
                j = 2 * C + jj
                sb_ = 4 if (2 * kv4 + jj) % 2 == 0 else 6
                sbk = f"pb{sb_}"
                top(lambda e, kv4=kv4, jj=jj, sb_=sb_, slot=slot:
                    e.matmul(out=pb[sb_][:, 256 * jj:256 * (jj + 1)],
                             lhsT=self.skw[:, 768 * slot + 192 * kv4 + 64 * jj:768 * slot + 192 * kv4 + 64 * jj + 128],
                             rhs=self.sqh.ap().rearrange("p (h n) -> p h n", h=16)
                             [:, 4 * kv4:4 * (kv4 + 1), 128 * C + 64 * jj:128 * C + 64 * (jj + 1)],
                             start=True, stop=True),
                    reads=[f"skw{slot}", "sqh"], writes=[sbk])
                sop(lambda e, kv4=kv4, jj=jj, sb_=sb_: e.activation(
                        out=self.ptb[:, 512 * kv4 + 256 * jj:512 * kv4 + 256 * (jj + 1)],
                        in_=pb[sb_][:, 256 * jj:256 * (jj + 1)],
                        func=AF.Exp, scale=RSQD),
                    reads=[sbk], writes=[f"ptb{kvh}"])
                mk = self.mask0 if j == 0 else self.maskg
                mkk = "mask0" if j == 0 else "maskg"
                vop(lambda e, kv4=kv4, jj=jj, mk=mk:
                    e.tensor_tensor(out=self.ptb.ap().rearrange("p (v j h n) -> p v j h n", v=4, j=2, h=4)[:, kv4, jj],
                                    in0=self.ptb.ap().rearrange("p (v j h n) -> p v j h n", v=4, j=2, h=4)[:, kv4, jj],
                                    in1=mk[:, :].unsqueeze(1).broadcast_to((128, 4, W)),
                                    op=ALU.mult),
                    reads=[f"ptb{kvh}", mkk], writes=[f"ptb{kvh}"])
        for kv4 in range(4):
            kvh = 4 * hh + kv4
            for jj in range(2):
                for h4 in range(4):
                    top(lambda e, kv4=kv4, jj=jj, h4=h4:
                        e.matmul(out=pb[5][64 * jj:64 * jj + 64, 4 * kv4 + h4:4 * kv4 + h4 + 1],
                                 lhsT=self.ptb[:, 512 * kv4 + 256 * jj + 64 * h4:512 * kv4 + 256 * jj + 64 * (h4 + 1)],
                                 rhs=self.onescol[:, 0:1], start=True, stop=True),
                        reads=[f"ptb{kvh}", "onesb"], writes=["pb5"])
        vop(lambda e: e.reciprocal(out=self.sswa[:, 0:16], in_=pb[5][:, 0:16]),
            reads=["pb5"], writes=["rzs"])
        vop(lambda e: e.tensor_scalar(out=self.sswa[:, 16:32], in0=self.sswa[:, 0:16],
                                      scalar1=0.5, scalar2=None, op0=ALU.mult),
            reads=["rzs"], writes=["rzs2"])
        for kv4 in range(4):
            kvh = 4 * hh + kv4
            ob = 2 + kv4 % 2
            pvb = 5 + kv4 % 2
            pok, ppk = f"pb{ob}", f"pb{pvb}"
            for h4 in range(4):
                hl = 4 * kv4 + h4
                top(lambda e, kv4=kv4, h4=h4:
                    e.matmul(out=pb[ob][:, 128 * h4:128 * (h4 + 1)],
                             lhsT=self.sct[:, 512 * kv4 + 128 * h4:512 * kv4 + 128 * (h4 + 1)],
                             rhs=self.svw[:, svo + 128 * (4 + kv4):svo + 128 * (5 + kv4)],
                             start=True, stop=False),
                    reads=["sct", vtk], writes=[pok])
                top(lambda e, kvh=kvh, h4=h4, hl=hl:
                    e.matmul(out=pb[ob][:, 128 * h4:128 * (h4 + 1)],
                             lhsT=self.qha[:, T * hl + 128 * C:T * hl + 128 * (C + 1)],
                             rhs=self.sbf[:, 128 * kvh:128 * (kvh + 1)],
                             start=False, stop=True),
                    reads=["qha", "sbf"], writes=[pok])
                for jj in range(2):
                    top(lambda e, kv4=kv4, h4=h4, jj=jj:
                        e.matmul(out=pb[pvb][64 * jj:64 * jj + 64, 128 * h4:128 * (h4 + 1)],
                                 lhsT=self.ptb[:, 512 * kv4 + 256 * jj + 64 * h4:512 * kv4 + 256 * jj + 64 * (h4 + 1)],
                                 rhs=self.svw[:, svo + 128 * (4 * jj + kv4):svo + 128 * (4 * jj + kv4 + 1)],
                                 start=True, stop=True),
                        reads=[f"ptb{kvh}", svk], writes=[ppk])
            top(lambda e, kv4=kv4, kvh=kvh: e.matmul(out=pb[4][:, 128 * kv4:128 * kv4 + 128],
                                   lhsT=self.khc[:, 128 * kv4:128 * (kv4 + 1)],
                                   rhs=self.svw[:, svo + 128 * (4 + kv4):svo + 128 * (5 + kv4)],
                                   start=True, stop=True),
                reads=[khk, vtk], writes=["pb4"])
            vop(lambda e, kv4=kv4: e.tensor_tensor(
                    out=self.qrw.ap().rearrange("p (h n) -> p h n", h=4),
                    in0=pb[ob].ap().rearrange("p (h n) -> p h n", h=4),
                    in1=self.scal.ap().rearrange("p (o h c) -> p o h c", o=2, c=NCH)
                    [:, 1, 4 * kv4:4 * (kv4 + 1), C:C + 1].broadcast_to((128, 4, 128)),
                    op=ALU.mult),
                reads=[pok, "sgla"], writes=["qrw"])
            vop(lambda e, kv4=kv4: e.tensor_tensor(
                    out=self.ahq[:, 0:512].rearrange("p (h n) -> p h n", h=4),
                    in0=pb[pvb].ap().rearrange("p (h n) -> p h n", h=4),
                    in1=self.sswa.rearrange("p (o h) -> p o h", o=2)
                    [:, 1, 4 * kv4:4 * (kv4 + 1)].unsqueeze(2).broadcast_to((128, 4, 128)),
                    op=ALU.mult),
                reads=[ppk, "rzs2"], writes=["ahq"])
            oslot = kv4 % 2
            vop(lambda e, oslot=oslot: e.tensor_tensor(out=self.omix[:, 512 * oslot:512 * (oslot + 1)],
                                          in0=self.qrw[:, :], in1=self.ahq[:, 0:512], op=ALU.add),
                reads=["qrw", "ahq"], writes=[f"omix{oslot}"])
            dma(self.ny, "y", "dom",
                self.attn_scr[128 * C:128 * (C + 1), 2048 * hh + 512 * kv4:2048 * hh + 512 * (kv4 + 1)],
                self.omix[:, 512 * oslot:512 * (oslot + 1)],
                reads=[f"omix{oslot}"], writes=["ascr2"])
            vop(lambda e, kvh=kvh, kv4=kv4: e.tensor_add(out=self.s32[:, 128 * kvh:128 * (kvh + 1)],
                                       in0=self.s32[:, 128 * kvh:128 * (kvh + 1)],
                                       in1=pb[4][:, 128 * kv4:128 * kv4 + 128]),
                reads=["pb4", "s32"], writes=["s32"])
            sop(lambda e, kvh=kvh: e.activation(out=self.sbf[:, 128 * kvh:128 * (kvh + 1)],
                                       in_=self.s32[:, 128 * kvh:128 * (kvh + 1)], func=AF.Copy),
                reads=["s32"], writes=["sbf"])


def _fixup_emit(self):
    pass


def _host_tables(p0):
    pos = np.arange(p0 - HLO, p0 + T, dtype=np.float64)
    half = np.arange(64, dtype=np.float64)
    inv = 1.0 / (10000.0 ** (2.0 * half / 128.0))
    dvec = inv[np.concatenate([np.arange(64), np.arange(64)])]
    ang = dvec[:, None] * pos[None, :]
    return np.cos(ang).astype(np.float32), np.sin(ang).astype(np.float32)


def _host_masks(first_half):
    a = np.arange(W)[None, :]
    c = np.arange(2 * W)[:, None]
    g = ((a <= c) & (c <= a + W)).astype(np.float32)
    m0 = g * (c >= W) if first_half else g
    return np.ascontiguousarray(m0.astype(np.float32)), g


_CACHE = {}


def kernel(L0_attn_hidden_states, L0_attn_Wq, L0_attn_Wk, L0_attn_Wv, L0_attn_Wo,
           L0_ffn_weight_q, L0_ffn_weight_scale, L0_ffn_bias):
    X = np.ascontiguousarray(L0_attn_hidden_states, dtype=np.float32)
    def pack(w, nt):
        n = w.shape[1] // nt
        return np.ascontiguousarray(
            w.reshape(32, 128, nt, n).transpose(2, 1, 0, 3))
    wk = pack(np.asarray(L0_attn_Wk, np.float32), 8)
    wv = pack(np.asarray(L0_attn_Wv, np.float32), 2)
    wq = pack(np.asarray(L0_attn_Wq, np.float32), 32)
    wo = pack(np.asarray(L0_attn_Wo, np.float32), 8)
    wf = pack(np.ascontiguousarray(L0_ffn_weight_q.T), 8)
    wsb = np.ascontiguousarray(np.broadcast_to(L0_ffn_weight_scale[None, :], (128, HID)), dtype=np.float32)
    bb = np.ascontiguousarray(np.broadcast_to(L0_ffn_bias[None, :], (128, HID)), dtype=np.float32)
    gm = np.tril(np.ones((128, 128), np.float32)).T
    gm = np.ascontiguousarray(gm)
    ident = np.eye(128).astype(BF)

    if "nc" not in _CACHE:
        kb = KB(bass.Bass("TRN2", target_bir_lowering=False))
        _CACHE["nc"] = kb.build()
    nc = _CACHE["nc"]

    in_maps = []
    zh = np.zeros((T, HID), np.float32)
    zhalo = np.zeros((HLO, HID), np.float32)
    for c in range(NCORES):
        b, h = c // 2, c % 2
        p0 = h * T
        halo = X[b, p0 - HLO:p0] if h == 1 else zhalo
        x = np.ascontiguousarray(np.concatenate([halo, X[b, p0:p0 + T]], axis=0))
        xh = np.ascontiguousarray(X[b, 0:T]) if h == 1 else zh
        cosf, sinf = _host_tables(p0)
        m0, mg = _host_masks(h == 0)
        in_maps.append({
            "x": x, "xh": xh, "wk": wk, "wv": wv, "wq": wq, "wo": wo, "wf": wf,
            "wsb": wsb, "bb": bb,
            "cosf": np.ascontiguousarray(cosf), "sinf": np.ascontiguousarray(sinf),
            "mask0": m0, "maskg": np.ascontiguousarray(mg), "gmask": gm, "ident": ident,
        })
    res = run_bass_kernel_spmd(nc, in_maps, core_ids=list(range(NCORES)))
    out = np.empty((4, 2048, HID), np.float32)
    for c in range(NCORES):
        b, h = c // 2, c % 2
        out[b, h * T:(h + 1) * T] = res.results[c]["out"]
    return out


# version 16 (final)
# speedup vs baseline: 1.0134x; 1.0134x over previous
import math
import numpy as np
import ml_dtypes
from contextlib import ExitStack
import concourse.bass as bass
import concourse.mybir as mybir
from concourse.bass_utils import run_bass_kernel_spmd

F32 = mybir.dt.float32
BF16 = mybir.dt.bfloat16
I8 = mybir.dt.int8
AF = mybir.ActivationFunctionType
ALU = mybir.AluOpType
AX = mybir.AxisListType

HID = 4096
DKV = 1024
DQ = 4096
D = 128
KVH = 8
HQ = 32
T = 1024
HLO = 64
TH = T + HLO
NCH = 8
W = 64
MAGIC = 12582912.0
RSQD = 1.0 / math.sqrt(128.0)
G16 = 1.0 / 16.0
EPS = 1e-5
NCORES = 8
BF = ml_dtypes.bfloat16


class Trk:
    def __init__(self, nc, es):
        self.nc = nc
        self.es = es
        self.sems = {}
        self.waited = {}
        self.w = {}
        self.r = {}

    def sem(self, name):
        if name not in self.sems:
            self.sems[name] = [self.es.enter_context(self.nc.semaphore(name)), 0]
        return self.sems[name]

    def _wait(self, eng, engname, semname, val):
        wt = self.waited.setdefault(engname, {})
        if wt.get(semname, 0) < val:
            eng.wait_ge(self.sems[semname][0], val)
            wt[semname] = val

    def dep(self, eng, engname, reads=(), writes=()):
        need = {}
        for k in reads:
            lw = self.w.get(k)
            if lw is not None:
                need[lw[0]] = max(need.get(lw[0], 0), lw[1])
        for k in writes:
            lw = self.w.get(k)
            if lw is not None:
                need[lw[0]] = max(need.get(lw[0], 0), lw[1])
            for (sn, sv) in self.r.get(k, ()):
                need[sn] = max(need.get(sn, 0), sv)
        for sn, sv in need.items():
            if sn.startswith("d"):
                sv = self.sems[sn][1]
            self._wait(eng, engname, sn, sv)

    def done(self, inst, semname, inc, reads=(), writes=()):
        s = self.sem(semname)
        s[1] += inc
        inst.then_inc(s[0], inc)
        for k in writes:
            self.w[k] = (semname, s[1])
            self.r[k] = []
        for k in reads:
            self.r.setdefault(k, []).append((semname, s[1]))


class KB:

    def __init__(self, nc, stop_after=None):
        self.nc = nc
        self.stop_after = stop_after
        self.es = ExitStack()
        self.trk = Trk(nc, self.es)
        self.scopes = []

    def sb(self, name, p, f, dt, scoped=True, scope_idx=None):
        cm = self.nc.sbuf_tensor(name, [p, f], dt)
        if scope_idx is not None and self.scopes:
            return self.scopes[scope_idx].enter_context(cm)
        if scoped and self.scopes:
            return self.scopes[-1].enter_context(cm)
        return self.es.enter_context(cm)

    def push_scope(self):
        self.scopes.append(ExitStack())

    def pop_scope(self):
        self.scopes.pop().close()
        self.barrier()

    def dma(self, q, qname, semname, outap, inap, reads=(), writes=(), transpose=False, accum=False):
        self.trk.dep(q, qname, reads=reads, writes=writes)
        if transpose:
            i = q.dma_start(out=outap, in_=inap, transpose=True)
        elif accum:
            i = q.dma_start(out=outap, in_=inap, accum_op=ALU.add)
        else:
            i = q.dma_start(out=outap, in_=inap)
        self.trk.done(i, semname, 16, reads=reads, writes=writes)

    def op(self, eng, engname, semname, fn, reads=(), writes=()):
        self.trk.dep(eng, engname, reads=reads, writes=writes)
        i = fn(eng)
        self.trk.done(i, semname, 1, reads=reads, writes=writes)

    def vop(self, fn, reads=(), writes=()):
        self.op(self.nv, "v", "tv", fn, reads, writes)

    def sop(self, fn, reads=(), writes=()):
        self.op(self.ns, "s", "ts", fn, reads, writes)

    def top(self, fn, reads=(), writes=()):
        self.op(self.nt, "t", "tt", fn, reads, writes)

    def barrier(self):
        t = self.trk
        for sn in list(t.sems):
            if t.sems[sn][1] > 0 and not sn.startswith("bar"):
                t._wait(self.ny, "y", sn, t.sems[sn][1])
        name = f"bar{len(t.sems)}"
        s = t.sem(name)
        self.ny.nop().then_inc(s[0], 1)
        s[1] += 1
        engs = [(self.nv, "v"), (self.ns, "s"), (self.nt, "t"), (self.ng, "g"), (self.ny, "y")]
        for eng, en in engs:
            t._wait(eng, en, name, 1)
        for en in list(t.waited):
            t.waited[en] = {name: 1}
        t.w = {k: (name, 1) for k in t.w}
        t.r = {}

    def build(self):
        nc = self.nc
        din = lambda name, shape, dt: nc.dram_tensor(name, shape, dt, kind="ExternalInput").ap()
        self.x_in = din("x", [TH, HID], F32)
        self.xh_in = din("xh", [T, HID], F32)
        self.wk_in = din("wk", [KVH, 128, 32, 128], F32)
        self.wv_in = din("wv", [2, 128, 32, 512], F32)
        self.wq_in = din("wq", [HQ, 128, 32, 128], F32)
        self.wo_in = din("wo", [8, 128, 32, 512], F32)
        self.wf_in = din("wf", [8, 128, 32, 512], I8)
        self.wsb_in = din("wsb", [128, HID], F32)
        self.bb_in = din("bb", [128, HID], F32)
        self.cosf_in = din("cosf", [128, TH], F32)
        self.sinf_in = din("sinf", [128, TH], F32)
        self.mask0_in = din("mask0", [128, W], F32)
        self.maskg_in = din("maskg", [128, W], F32)
        self.gmask_in = din("gmask", [128, 128], F32)
        self.ident_in = din("ident", [128, 128], BF16)
        self.out = nc.dram_tensor("out", [T, HID], F32, kind="ExternalOutput").ap()

        dsc = lambda name, shape, dt: nc.dram_tensor(name, shape, dt, kind="Internal").ap()
        self.ln_scr = dsc("ln_scr", [TH, HID], BF16)
        self.lnh_scr = dsc("lnh_scr", [T, HID], BF16)
        self.v_scr = dsc("v_scr", [TH, DKV], BF16)
        self.a_scr = dsc("a_scr", [KVH * 128, T], F32)
        self.xq_scr = dsc("xq_scr", [T, HID], BF16)
        self.attn_scr = dsc("attn_scr", [T, DQ], BF16)
        self.kt_scr = dsc("kt_scr", [KVH, 128, T], BF16)
        self.kh_scr = dsc("kh_scr", [KVH, 128, T], BF16)
        self.khh_scr = dsc("khh_scr", [KVH, 128, T], BF16)
        self.sk_scr = dsc("sk_scr", [KVH, 128, TH], BF16)

        self.cpool = self.sb("cpool", 128, 16, F32, scoped=False)
        self.onescol = self.sb("onescol", 128, 1, BF16, scoped=False)
        self.masks = self.sb("s_masks", 128, 2 * W, F32, scoped=False)
        self.mask0 = self.masks[:, 0:W]
        self.maskg = self.masks[:, W:2 * W]
        self.gmask = self.sb("s_gmask", 128, 128, F32, scoped=False)
        self.ident = self.sb("s_ident", 128, 128, BF16, scoped=False)
        self.small = self.sb("small", 128, 160, F32, scoped=False)
        self.stat = self.small[:, 0:32]
        self.colb = self.small[:, 32:40]
        self.aend = self.small[:, 40:48]
        self.rzk = self.small[:, 48:112]
        self.xsc = self.small[:, 112:120]
        self.sswa = self.small[:, 120:152]
        self.s32 = self.sb("s32", 128, KVH * 128, F32, scoped=False)
        self.sbf = self.sb("sbf", 128, KVH * 128, BF16, scoped=False)

        self.pb = [self.es.enter_context(nc.psum_tensor(f"pb{i}", [128, 512], F32)) for i in range(7)]
        self.pbt = self.es.enter_context(nc.psum_tensor("pbt", [128, 1024], BF16))

        blk = self.es.enter_context(nc.Block())

        @blk.sync
        def _(root):
            self.nv, self.ns, self.nt, self.ng, self.ny = nc.vector, nc.scalar, nc.tensor, nc.gpsimd, root
            self.emit()

        self.es.close()
        return nc

    def emit(self):
        vop, sop, top, dma = self.vop, self.sop, self.top, self.dma
        nv, ns, nt, ng, ny = self.nv, self.ns, self.nt, self.ng, self.ny
        pb, pbt = self.pb, self.pbt

        vop(lambda e: e.memset(self.cpool[:, :], 0.0), writes=["cpool"])
        vop(lambda e: e.memset(self.cpool[:, 8:9], 1.0), writes=["ones32", "cpool"])
        vop(lambda e: e.memset(self.cpool[:, 9:10], EPS), writes=["cEPS", "cpool"])
        vop(lambda e: e.memset(self.onescol[:, :], 1.0), writes=["onesb"])
        dma(ng, "g", "dck", self.mask0[:, :], self.mask0_in, writes=["mask0"])
        dma(ng, "g", "dck", self.maskg[:, :], self.maskg_in, writes=["maskg"])
        dma(ng, "g", "dck", self.gmask[:, :], self.gmask_in, writes=["gmask"])
        dma(ng, "g", "dck", self.ident[:, :], self.ident_in, writes=["ident"])


        self.push_scope()
        self.lnq = self.sb("lnq", 128, 32 * TH, BF16)
        self.wkb = self.sb("wkb", 128, 2 * 32 * 128, BF16)
        self.cosf = self.sb("s_cosf", 128, TH, F32)
        self.sinf = self.sb("s_sinf", 128, TH, F32)
        dma(self.ng, "g", "dck", self.cosf[:, :], self.cosf_in, writes=["cosf"])
        dma(self.ng, "g", "dck", self.sinf[:, :], self.sinf_in, writes=["sinf"])
        sc1 = self.sb("bc1", 128, TH, F32)
        sc2 = self.sb("bc2", 128, TH, F32)
        sc3 = self.sb("bc3", 128, T, F32)
        sc4 = self.sb("bc4", 128, T, F32)
        self.bscr = (sc1, sc2, sc3, sc4)

        self.push_scope()
        scr1 = self.sb("pa1", 128, 2 * HID, F32)
        scr2d = self.sb("pa2", 128, HID, F32)
        scrb = self.sb("pab", 128, 2 * HID, BF16)
        self.emit_ln(self.xh_in, T, self.lnh_scr, "H", scr1, scr2d, scrb)
        self.emit_ln(self.x_in, TH, self.ln_scr, "L", scr1, scr2d, scrb)
        self.pop_scope()
        if self.stop_after == "A":
            self.pop_scope()
            return

        vop(lambda e: e.memset(self.s32[:, :], 0.0), writes=["s32"])
        self.push_scope()
        self.wvb = self.sb("wvb", 128, 32 * 512, BF16)
        self.kfm = self.sb("kfm", 128, 4 * TH, F32)
        self.vtok = self.sb("vtok", 128, 9 * DKV, BF16)
        self.stg = self.sb("stg", 128, T, BF16)
        self.stg2 = self.sb("stg2", 128, TH, BF16)
        self.scb = self.sb("bcb", 128, 256, BF16)
        self.khc2 = self.sb("khc2", 128, KVH * 128, BF16)

        self.emit_kvphase(self.lnh_scr, T, "H", halo=True)
        self.load_lnq(self.ln_scr, TH, "L")
        self.emit_halostate()
        for kvh in range(KVH):
            sop(lambda e, kvh=kvh: e.activation(out=self.stat[:, 12:13], in_=self.aend[:, kvh:kvh + 1],
                                                func=AF.Exp, scale=G16),
                reads=["aend"], writes=["cbE"])
            sop(lambda e, kvh=kvh: e.activation(out=self.s32[:, kvh * 128:(kvh + 1) * 128],
                                                in_=self.s32[:, kvh * 128:(kvh + 1) * 128],
                                                func=AF.Copy, scale=self.stat[:, 12:13]),
                reads=["cbE", "s32"], writes=["s32"])
        sop(lambda e: e.activation(out=self.sbf[:, :], in_=self.s32[:, :], func=AF.Copy),
            reads=["s32"], writes=["sbf"])
        if self.stop_after == "B":
            self.pop_scope()
            self.pop_scope()
            return
        self.emit_kvphase(self.ln_scr, TH, "L", halo=False, skip_lnq=True)
        self.pop_scope()
        if self.stop_after == "C":
            self.pop_scope()
            return

        self.push_scope()
        self.qha = self.sb("qha", 128, 16 * T, BF16)
        self.sqh = self.sb("sqh", 128, 16 * T, BF16)
        self.svw = self.sb("svw", 128, 2 * 8 * 128, BF16)
        self.sct = self.sb("sct", 128, 4 * 512, BF16)
        self.ptb = self.sb("ptb", 128, 4 * 512, BF16)
        self.omix = self.sb("omix", 128, 2 * 512, BF16)
        self.ahq = self.sb("ahq", 128, 512, F32)
        self.scal = self.sb("scal", 128, 256, F32)
        self.qrw = self.sb("qrw", 128, 512, F32)
        self.ktc = self.sb("ktc", 128, 2 * 4 * 128, BF16)
        self.khc = self.sb("khc", 128, 4 * 128, BF16)
        self.skw = self.sb("skw", 128, 2 * 4 * 192, BF16)
        for hh in range(2):
            self.emit_qheads(hh)
            if self.stop_after and self.stop_after.startswith("Q0"):
                break
            self.emit_chunk_loads(0, hh)
            for C in range(NCH):
                if C + 1 < NCH:
                    self.emit_chunk_loads(C + 1, hh)
                self.emit_chunk(C, hh)
                if self.stop_after in ("K0", "K1") and C >= (0 if self.stop_after == "K0" else 1):
                    break
            if self.stop_after in ("K0", "K1"):
                break
        self.pop_scope()
        self.pop_scope()
        if self.stop_after == "D":
            return

        self.push_scope()
        afm = self.sb("afm", 128, 32 * T, BF16)
        wob = self.sb("wob", 128, 2 * 32 * 512, BF16)
        xrow = self.sb("xrow", 128, 2 * 512, F32)
        x2p = self.sb("x2p", 128, 2 * 512, F32)
        x2sb = self.sb("x2sb", 128, HID, F32)
        qf1 = self.sb("qf1", 128, HID, F32)
        qf2 = self.sb("qf2", 128, HID, F32)
        qfb = self.sb("qfb", 128, HID, BF16)
        for k in range(32):
            dma(ny, "y", "dtr", afm[:, T * k:T * k + T],
                self.attn_scr[0:T, 128 * k:128 * (k + 1)],
                reads=["ascr2"], writes=[f"afm{k}"], transpose=True)
        for hs in range(8):
            ws = hs % 2
            dma(ng, "g", f"dwo{ws}",
                wob.ap().rearrange("p (s k n) -> p s k n", s=2, k=32)[:, ws],
                self.wo_in[hs], writes=[f"wob{ws}"])
            for tc in range(8):
                bank = tc % 2
                pk = f"pb{bank}"
                sl = tc % 2
                dma(ny, "y", f"dxr{sl}", xrow[:, 512 * sl:512 * (sl + 1)],
                    self.x_in[HLO + 128 * tc:HLO + 128 * (tc + 1), 512 * hs:512 * (hs + 1)],
                    writes=[f"xrow{sl}"])
                for kk in range(32):
                    top(lambda e, kk=kk, tc=tc, bank=bank, ws=ws:
                        e.matmul(out=pb[bank][:, 0:512],
                                 lhsT=afm[:, T * kk + 128 * tc:T * kk + 128 * (tc + 1)],
                                 rhs=wob[:, (ws * 32 + kk) * 512:(ws * 32 + kk + 1) * 512],
                                 start=(kk == 0), stop=(kk == 31)),
                        reads=[f"afm{kk}", f"wob{ws}"], writes=[pk])
                vop(lambda e, bank=bank, sl=sl: e.tensor_add(out=x2p[:, 512 * sl:512 * (sl + 1)],
                                                             in0=pb[bank][:, 0:512],
                                                             in1=xrow[:, 512 * sl:512 * (sl + 1)]),
                    reads=[pk, f"xrow{sl}"], writes=[f"x2p{sl}"])
                dma(ny, "y", "dx2w", self.out[128 * tc:128 * (tc + 1), 512 * hs:512 * (hs + 1)],
                    x2p[:, 512 * sl:512 * (sl + 1)], reads=[f"x2p{sl}"], writes=["x2scr"])
                if hs == 7:
                    self.emit_quant(tc, x2sb, qf1, qf2, qfb)
        self.pop_scope()
        if self.stop_after == "E":
            return

        self.push_scope()
        xqf = self.sb("xqf", 128, 32 * T, BF16)
        wfb = self.sb("wfb", 128, 2 * 32 * 512, BF16)
        wsb = self.sb("s_wsb", 128, HID, F32)
        bbs = self.sb("bbs", 128, HID, F32)
        g1 = self.sb("g1", 128, 512, F32)
        g2 = self.sb("g2", 128, 512, F32)
        g3 = self.sb("g3", 128, 512, F32)
        dma(ng, "g", "dck", wsb[:, :], self.wsb_in, writes=["wsb"])
        dma(ng, "g", "dck", bbs[:, :], self.bb_in, writes=["bb"])
        for k in range(32):
            dma(ny, "y", "dtr2", xqf[:, T * k:T * k + T],
                self.xq_scr[0:T, 128 * k:128 * (k + 1)],
                reads=["xqscr"], writes=[f"xqf{k}"], transpose=True)
        nst = 0
        for os_ in range(8):
            ws = os_ % 2
            dma(ng, "g", f"dwf{ws}",
                wfb.ap().rearrange("p (s k n) -> p s k n", s=2, k=32)[:, ws],
                self.wf_in[os_], writes=[f"wfb{ws}"])
            for tc in range(8):
                bank = tc % 2
                pk = f"pb{bank}"
                for kk in range(32):
                    top(lambda e, kk=kk, tc=tc, bank=bank, ws=ws:
                        e.matmul(out=pb[bank][:, 0:512],
                                 lhsT=xqf[:, T * kk + 128 * tc:T * kk + 128 * (tc + 1)],
                                 rhs=wfb[:, (ws * 32 + kk) * 512:(ws * 32 + kk + 1) * 512],
                                 start=(kk == 0), stop=(kk == 31)),
                        reads=[f"xqf{kk}", f"wfb{ws}"], writes=[pk])
                sop(lambda e, tc=tc, bank=bank: e.activation(out=g1[:, :], in_=pb[bank][:, 0:512],
                                                             func=AF.Copy,
                                                             scale=self.xsc[:, tc:tc + 1]),
                    reads=[pk, "xsc"], writes=["g1"])
                vop(lambda e, os_=os_: e.tensor_mul(out=g2[:, :], in0=g1[:, :],
                                                    in1=wsb[:, 512 * os_:512 * (os_ + 1)]),
                    reads=["g1", "wsb"], writes=["g2"])
                vop(lambda e, os_=os_: e.tensor_tensor(out=g2[:, :], in0=g2[:, :],
                                                       in1=bbs[:, 512 * os_:512 * (os_ + 1)],
                                                       op=ALU.add),
                    reads=["g2", "bb"], writes=["g2"])
                dma(ng, "g", "dout", self.out[128 * tc:128 * (tc + 1), 512 * os_:512 * (os_ + 1)],
                    g2[:, :], reads=["g2"], writes=["outw", "x2scr"], accum=True)
                nst += 1
        self.ng.wait_ge(self.trk.sems["dout"][0], 16 * nst)
        self.ny.wait_ge(self.trk.sems["dout"][0], 16 * nst)
        self.pop_scope()


    def emit_quant(self, tc, x2sb, qf1, qf2, qfb):
        vop, sop, dma = self.vop, self.sop, self.dma
        st = self.stat
        dma(self.ny, "y", "dx2r", x2sb[:, :], self.out[128 * tc:128 * (tc + 1), :],
            reads=["x2scr"], writes=["x2sb"])
        sop(lambda e: e.activation(out=qf1[:, :], in_=x2sb[:, :], func=AF.Square,
                                   accum_out=st[:, 0:1]),
            reads=["x2sb"], writes=["f0", "fs0"])
        vop(lambda e: e.reduce_sum(out=st[:, 1:2], in_=x2sb[:, :], axis=AX.X),
            reads=["x2sb"], writes=["fs1"])
        vop(lambda e: e.tensor_scalar(out=st[:, 2:3], in0=st[:, 1:2], scalar1=1.0 / HID,
                                      scalar2=None, op0=ALU.mult),
            reads=["fs1"], writes=["fs2"])
        vop(lambda e: e.tensor_scalar(out=st[:, 3:4], in0=st[:, 2:3], scalar1=st[:, 2:3],
                                      scalar2=None, op0=ALU.mult),
            reads=["fs2"], writes=["fs3"])
        vop(lambda e: e.scalar_tensor_tensor(out=st[:, 4:5], in0=st[:, 0:1], scalar=1.0 / HID,
                                             in1=st[:, 3:4], op0=ALU.mult, op1=ALU.subtract),
            reads=["fs0", "fs3"], writes=["fs4"])
        sop(lambda e: e.activation(out=st[:, 5:6], in_=st[:, 4:5], func=AF.Sqrt,
                                   bias=self.cpool[:, 9:10]),
            reads=["fs4", "cEPS"], writes=["fs5"])
        vop(lambda e: e.reciprocal(out=st[:, 6:7], in_=st[:, 5:6]),
            reads=["fs5"], writes=["fs6"])
        vop(lambda e: e.tensor_scalar(out=qf1[:, :], in0=x2sb[:, :],
                                      scalar1=st[:, 2:3], scalar2=st[:, 6:7],
                                      op0=ALU.subtract, op1=ALU.mult),
            reads=["x2sb", "fs2", "fs6"], writes=["f0"])
        vop(lambda e: e.reduce_max(out=st[:, 8:9], in_=qf1[:, :], axis=AX.X,
                                   apply_absolute_value=True),
            reads=["f0"], writes=["fs8"])
        vop(lambda e, tc=tc: e.tensor_scalar(out=self.xsc[:, tc:tc + 1], in0=st[:, 8:9],
                                             scalar1=1e-8, scalar2=1.0 / 127.0,
                                             op0=ALU.max, op1=ALU.mult),
            reads=["fs8"], writes=["xsc"])
        vop(lambda e, tc=tc: e.reciprocal(out=st[:, 9:10], in_=self.xsc[:, tc:tc + 1]),
            reads=["xsc"], writes=["fs9"])
        sop(lambda e: e.activation(out=qf2[:, :], in_=qf1[:, :], func=AF.Copy, scale=st[:, 9:10]),
            reads=["f0", "fs9"], writes=["f2"])
        vop(lambda e: e.tensor_scalar(out=qf1[:, :], in0=qf2[:, :], scalar1=MAGIC, scalar2=None,
                                      op0=ALU.add),
            reads=["f2"], writes=["f0"])
        vop(lambda e: e.tensor_scalar(out=qf2[:, :], in0=qf1[:, :], scalar1=MAGIC, scalar2=127.0,
                                      op0=ALU.subtract, op1=ALU.min),
            reads=["f0"], writes=["f2"])
        vop(lambda e: e.tensor_scalar(out=qfb[:, :], in0=qf2[:, :], scalar1=-128.0, scalar2=None,
                                      op0=ALU.max),
            reads=["f2"], writes=["fb"])
        dma(self.ny, "y", "dxq", self.xq_scr[128 * tc:128 * (tc + 1), :], qfb[:, :],
            reads=["fb"], writes=["xqscr"])

    def emit_ln(self, src_ap, nrows, dst_scr, tag, scr1, scr2d, scrb):
        vop, sop, dma = self.vop, self.sop, self.dma
        st = self.stat
        r0, ti = 0, 0
        while r0 < nrows:
            rr = min(128, nrows - r0)
            sl = ti % 2
            xk = f"lnx{sl}"
            bk = f"lnb{sl}"
            x1 = scr1[:, HID * sl:HID * (sl + 1)]
            xb = scrb[:, HID * sl:HID * (sl + 1)]
            dma(self.ny, "y", "dlx", x1[0:rr, :], src_ap[r0:r0 + rr, :], writes=[xk])
            sop(lambda e, rr=rr, x1=x1: e.activation(out=scr2d[0:rr, :],
                                              in_=x1[0:rr, :], func=AF.Square,
                                              accum_out=st[0:rr, 0:1]),
                reads=[xk], writes=["lsq", "l0"])
            vop(lambda e, rr=rr, x1=x1: e.reduce_sum(out=st[0:rr, 1:2], in_=x1[0:rr, :], axis=AX.X),
                reads=[xk], writes=["l1"])
            vop(lambda e, rr=rr: e.tensor_scalar(out=st[0:rr, 2:3], in0=st[0:rr, 1:2],
                                                 scalar1=1.0 / HID, scalar2=None, op0=ALU.mult),
                reads=["l1"], writes=["l2"])
            vop(lambda e, rr=rr: e.tensor_scalar(out=st[0:rr, 3:4], in0=st[0:rr, 2:3],
                                                 scalar1=st[0:rr, 2:3], scalar2=None, op0=ALU.mult),
                reads=["l2"], writes=["l3"])
            vop(lambda e, rr=rr: e.scalar_tensor_tensor(out=st[0:rr, 4:5], in0=st[0:rr, 0:1],
                                                        scalar=1.0 / HID, in1=st[0:rr, 3:4],
                                                        op0=ALU.mult, op1=ALU.subtract),
                reads=["l0", "l3"], writes=["l4"])
            sop(lambda e, rr=rr: e.activation(out=st[0:rr, 5:6], in_=st[0:rr, 4:5], func=AF.Sqrt,
                                              bias=self.cpool[0:rr, 9:10]),
                reads=["l4", "cEPS"], writes=["l5"])
            vop(lambda e, rr=rr: e.reciprocal(out=st[0:rr, 6:7], in_=st[0:rr, 5:6]),
                reads=["l5"], writes=["l6"])
            vop(lambda e, rr=rr, x1=x1, xb=xb: e.tensor_scalar(out=xb[0:rr, :], in0=x1[0:rr, :],
                                                 scalar1=st[0:rr, 2:3], scalar2=st[0:rr, 6:7],
                                                 op0=ALU.subtract, op1=ALU.mult),
                reads=[xk, "l2", "l6"], writes=[bk])
            dma(self.ny, "y", "dlw", dst_scr[r0:r0 + rr, :], xb[0:rr, :],
                reads=[bk], writes=[f"{tag}scr"])
            r0 += rr
            ti += 1

    def load_lnq(self, scr_ap, ttot, tag):
        for k in range(32):
            q, qn, sem = ((self.ny, "y", "dtr") if k % 2 == 0 else (self.ns, "s", "dts"))
            self.dma(q, qn, sem, self.lnq[:, TH * k:TH * k + ttot],
                     scr_ap[0:ttot, 128 * k:128 * (k + 1)],
                     reads=[f"{tag}scr"], writes=["lnq"], transpose=True)

    def emit_kvphase(self, scr_ap, ttot, tag, halo, skip_lnq=False):
        vop, sop, top, dma = self.vop, self.sop, self.top, self.dma
        pb = self.pb
        if not skip_lnq:
            self.load_lnq(scr_ap, ttot, tag)
        tsl = [(0, 512), (512, 512), (1024, 64)] if ttot == TH else [(0, 512), (512, 512)]
        for half in range(2):
            vg = half
            dma(self.ng, "g", "dwv", self.wvb.ap().rearrange("p (k n) -> p k n", k=32),
                self.wv_in[vg], writes=["wvb"])
            if half == 0:
                for kvh in range(KVH):
                    ws_ = kvh % 2
                    dma(self.ng, "g", f"dwk{ws_}",
                        self.wkb.ap().rearrange("p (s k n) -> p s k n", s=2, k=32)[:, ws_],
                        self.wk_in[kvh], writes=[f"wkb{ws_}"])
                    kv4 = kvh % 4
                    for (t0, tn) in tsl:
                        bank = (kvh + t0 // 512) % 2
                        pk = f"pb{bank}"
                        for kk in range(32):
                            top(lambda e, kk=kk, t0=t0, tn=tn, bank=bank, ws_=ws_:
                                e.matmul(out=pb[bank][:, 0:tn],
                                         lhsT=self.wkb[:, (ws_ * 32 + kk) * 128:(ws_ * 32 + kk + 1) * 128],
                                         rhs=self.lnq[:, TH * kk + t0:TH * kk + t0 + tn],
                                         start=(kk == 0), stop=(kk == 31)),
                                reads=["lnq", f"wkb{ws_}"], writes=[pk])
                        sop(lambda e, kvh=kvh, t0=t0, tn=tn, bank=bank:
                            e.activation(out=self.kfm[:, (kvh % 4) * ttot + t0:(kvh % 4) * ttot + t0 + tn],
                                         in_=pb[bank][:, 0:tn], func=AF.Copy),
                            reads=[pk], writes=[f"kfm{kvh % 4}"])
                    if kvh == 3 or kvh == 7:
                        for kv4b in range(4):
                            self.emit_kchain(4 * (kvh // 4) + kv4b, kv4b, ttot, tag, halo)
            tc0 = 0
            while tc0 < ttot:
                tcn = min(128, ttot - tc0)
                if tag == "L":
                    if tc0 < HLO:
                        tcn = min(tcn, HLO - tc0)
                    else:
                        tcn = min(tcn, 128 - (tc0 - HLO) % 128)
                bank = 2 + (tc0 // 128) % 2
                pk = f"pb{bank}"
                for kk in range(32):
                    top(lambda e, kk=kk, tc0=tc0, tcn=tcn, bank=bank:
                        e.matmul(out=pb[bank][0:tcn, 0:512],
                                 lhsT=self.lnq[:, TH * kk + tc0:TH * kk + tc0 + tcn],
                                 rhs=self.wvb[:, 512 * kk:512 * (kk + 1)],
                                 start=(kk == 0), stop=(kk == 31)),
                        reads=["lnq", "wvb"], writes=[pk])
                if tag == "L":
                    vk = "vstg"
                    sop(lambda e, tcn=tcn, bank=bank:
                        e.activation(out=self.stg.ap().bitcast(BF16)[0:tcn, 0:512],
                                     in_=pb[bank][0:tcn, 0:512], func=AF.Copy),
                        reads=[pk], writes=[vk])
                    dma(self.ny, "y", "dvs",
                        self.v_scr[tc0:tc0 + tcn, 512 * vg:512 * (vg + 1)],
                        self.stg.ap().bitcast(BF16)[0:tcn, 0:512],
                        reads=[vk], writes=["vscr"])
                else:
                    tile, ro = 1 + tc0 // 128, tc0 % 128
                    vk = f"vt{tile}"
                    sop(lambda e, tile=tile, ro=ro, tcn=tcn, vg=vg, bank=bank:
                        e.activation(out=self.vtok.ap().rearrange("p (c n) -> p c n", c=9)
                                     [ro:ro + tcn, tile, 512 * vg:512 * (vg + 1)],
                                     in_=pb[bank][0:tcn, 0:512], func=AF.Copy),
                        reads=[pk], writes=[vk])
                tc0 += tcn


    def emit_kchain(self, kvh, kv4, ttot, tag, halo):
        vop, sop, top, dma = self.vop, self.sop, self.top, self.dma
        pb, pbt = self.pb, self.pbt
        sc1, sc2, sc3, sc4 = self.bscr
        scb = self.scb
        kk_ = f"kfm{kv4}"
        off = kv4 * ttot
        g0 = HLO if tag == "L" else 0
        kfm = self.kfm
        sop(lambda e: e.activation(out=sc1[:, 0:T], in_=kfm[:, off + g0:off + g0 + T], func=AF.Abs),
            reads=[kk_], writes=["c1"])
        sop(lambda e: e.activation(out=sc1[:, 0:T], in_=sc1[:, 0:T], func=AF.Exp, scale=-1.0),
            reads=["c1"], writes=["c1"])
        sop(lambda e: e.activation(out=sc1[:, 0:T], in_=sc1[:, 0:T], func=AF.Ln, bias=1.0),
            reads=["c1"], writes=["c1"])
        vop(lambda e: e.tensor_scalar_min(out=sc2[:, 0:T], in0=kfm[:, off + g0:off + g0 + T], scalar1=0.0),
            reads=[kk_], writes=["c2"])
        vop(lambda e: e.tensor_sub(out=sc2[:, 0:T], in0=sc2[:, 0:T], in1=sc1[:, 0:T]),
            reads=["c1", "c2"], writes=["c2"])
        vop(lambda e: e.tensor_tensor_scan(out=sc3[:, 0:T], data0=sc2[:, 0:T], data1=sc2[:, 0:T],
                                           initial=0.0, op0=ALU.add, op1=ALU.bypass),
            reads=["c2"], writes=["c3"])
        if halo:
            vop(lambda e, kvh=kvh: e.tensor_copy(out=self.aend[:, kvh:kvh + 1], in_=sc3[:, T - 1:T]),
                reads=["c3"], writes=["aend"])
        else:
            dma(self.ny, "y", "das", self.a_scr[128 * kvh:128 * (kvh + 1), :], sc3[:, 0:T],
                reads=["c3"], writes=["ascr"])
        vop(lambda e: e.scalar_tensor_tensor(out=sc4[:, 0:T], in0=sc3[:, 0:T], scalar=-G16,
                                             in1=kfm[:, off + g0:off + g0 + T],
                                             op0=ALU.mult, op1=ALU.add),
            reads=["c3", kk_], writes=["c4"])
        if not halo:
            sop(lambda e: e.activation(out=self.stg2[:, 0:T], in_=sc4[:, 0:T], func=AF.Exp),
                reads=["c4"], writes=["kstg"])
            dma(self.ny, "y", "dkt", self.kt_scr[kvh], self.stg2[:, 0:T],
                reads=["kstg"], writes=["ktscr"])
        sop(lambda e: e.activation(out=sc1[:, 0:T], in_=kfm[:, off + g0:off + g0 + T], func=AF.Exp),
            reads=[kk_], writes=["c1"])
        for c in range(NCH):
            top(lambda e, c=c: e.matmul(out=pb[4][:, c:c + 1],
                                        lhsT=sc1[:, 128 * c:128 * (c + 1)],
                                        rhs=self.cpool[:, 8:9], start=True, stop=True),
                reads=["c1", "ones32"], writes=["pb4"])
        vop(lambda e, kvh=kvh: e.reciprocal(out=self.rzk[:, kvh * NCH:(kvh + 1) * NCH],
                                            in_=pb[4][:, 0:NCH]),
            reads=["pb4"], writes=[f"rzk{kvh}"])
        vop(lambda e: e.tensor_scalar(out=self.colb[:, 0:8],
                                      in0=sc3[:, 0:T].rearrange("p (c n) -> p c n", c=8)[:, :, 127],
                                      scalar1=G16, scalar2=None, op0=ALU.mult),
            reads=["c3"], writes=["cb1"])
        for c in range(NCH):
            par = c % 2
            sop(lambda e, c=c, par=par: e.activation(out=scb[:, 128 * par:128 * (par + 1)],
                                            in_=sc4[:, 128 * c:128 * (c + 1)], func=AF.Exp,
                                            bias=self.colb[:, c:c + 1]),
                reads=["c4", "cb1"], writes=[f"c2b{par}"])
            top(lambda e, par=par: e.transpose(out=pbt[:, 128 * par:128 * (par + 1)],
                                               in_=scb[:, 128 * par:128 * (par + 1)],
                                               identity=self.ident[:, :]),
                reads=[f"c2b{par}", "ident"], writes=[f"pbtk{par}"])
            sop(lambda e, c=c, kvh=kvh, par=par:
                e.activation(out=self.stg2[:, 128 * c:128 * (c + 1)],
                             in_=pbt[:, 128 * par:128 * (par + 1)], func=AF.Copy,
                             scale=self.rzk[:, kvh * NCH + c:kvh * NCH + c + 1]),
                reads=[f"pbtk{par}", f"rzk{kvh}"], writes=["kstg"])
        if halo:
            dma(self.ny, "y", "dkh2", self.khh_scr[kvh], self.stg2[:, 0:T],
                reads=["kstg"], writes=["khhscr"])
        else:
            dma(self.ny, "y", "dkh", self.kh_scr[kvh], self.stg2[:, 0:T],
                reads=["kstg"], writes=["khscr"])
            vop(lambda e: e.tensor_mul(out=sc1[:, 0:ttot], in0=kfm[:, off:off + ttot],
                                       in1=self.cosf[:, 0:ttot]),
                reads=[kk_, "cosf"], writes=["c1"])
            vop(lambda e: e.tensor_scalar(out=sc2[0:64, 0:ttot], in0=kfm[64:128, off:off + ttot],
                                          scalar1=-1.0, scalar2=None, op0=ALU.mult),
                reads=[kk_], writes=["c2"])
            vop(lambda e: e.tensor_copy(out=sc2[64:128, 0:ttot], in_=kfm[0:64, off:off + ttot]),
                reads=[kk_], writes=["c2"])
            vop(lambda e: e.tensor_mul(out=sc2[:, 0:ttot], in0=sc2[:, 0:ttot], in1=self.sinf[:, 0:ttot]),
                reads=["c2", "sinf"], writes=["c2"])
            vop(lambda e, ttot=ttot:
                e.tensor_tensor(out=self.stg2[:, 0:ttot],
                                in0=sc1[:, 0:ttot], in1=sc2[:, 0:ttot], op=ALU.add),
                reads=["c1", "c2"], writes=["kstg"])
            dma(self.ny, "y", "dsk", self.sk_scr[kvh][:, 0:ttot], self.stg2[:, 0:ttot],
                reads=["kstg"], writes=["skscr"])

    def emit_halostate(self):
        vop, top, dma = self.vop, self.top, self.dma
        pb = self.pb
        for c in range(NCH):
            dma(self.ny, "y", "dkh3", self.khc2[:, :],
                self.khh_scr.rearrange("h p t -> p h t")[:, :, 128 * c:128 * (c + 1)],
                reads=["khhscr"], writes=["khc2"])
            for kvh in range(KVH):
                bank = 5 + kvh % 2
                pk = f"pb{bank}"
                top(lambda e, kvh=kvh, c=c, bank=bank:
                    e.matmul(out=pb[bank][:, 0:128],
                             lhsT=self.khc2[:, 128 * kvh:128 * (kvh + 1)],
                             rhs=self.vtok.ap().rearrange("p (t n) -> p t n", t=9)
                             [:, 1 + c, 128 * kvh:128 * (kvh + 1)],
                             start=True, stop=True),
                    reads=["khc2", f"vt{1 + c}"], writes=[pk])
                vop(lambda e, kvh=kvh, bank=bank: e.tensor_add(out=self.s32[:, 128 * kvh:128 * (kvh + 1)],
                                                    in0=self.s32[:, 128 * kvh:128 * (kvh + 1)],
                                                    in1=pb[bank][:, 0:128]),
                    reads=[pk, "s32"], writes=["s32"])

    def emit_qheads(self, hh):
        vop, sop, top, dma = self.vop, self.sop, self.top, self.dma
        pb = self.pb
        sc1, sc2, sc3, sc4 = self.bscr
        qrw = self.qrw
        for hl in range(16):
            head = 16 * hh + hl
            kvh = head // 4
            ws = head % 2
            dma(self.ng, "g", f"dwq{ws}",
                self.wkb.ap().rearrange("p (s k n) -> p s k n", s=2, k=32)[:, ws],
                self.wq_in[head], writes=[f"wkb{ws}"])
            for tp in range(2):
                t0 = 512 * tp
                dma(self.ny, "y", "dah", self.ahq[:, :],
                    self.a_scr[128 * kvh:128 * (kvh + 1), t0:t0 + 512],
                    reads=["ascr"], writes=["ahq"])
                bank = 2 * (hl % 2) + tp
                pk = f"pb{bank}"
                for kk in range(32):
                    top(lambda e, kk=kk, t0=t0, bank=bank, ws=ws:
                        e.matmul(out=pb[bank][:, 0:512],
                                 lhsT=self.wkb[:, (ws * 32 + kk) * 128:(ws * 32 + kk + 1) * 128],
                                 rhs=self.lnq[:, TH * kk + HLO + t0:TH * kk + HLO + t0 + 512],
                                 start=(kk == 0), stop=(kk == 31)),
                        reads=["lnq", f"wkb{ws}"], writes=[pk])
                sop(lambda e, bank=bank: e.activation(out=qrw[:, :], in_=pb[bank][:, 0:512],
                                                      func=AF.Copy),
                    reads=[pk], writes=["qrw"])
                vop(lambda e: e.scalar_tensor_tensor(out=sc1[:, 0:512],
                                                     in0=self.ahq[:, :],
                                                     scalar=G16, in1=qrw[:, :],
                                                     op0=ALU.mult, op1=ALU.add),
                    reads=["qrw", "ahq"], writes=["c1"])
                sop(lambda e, hl=hl, t0=t0:
                    e.activation(out=self.qha[:, T * hl + t0:T * hl + t0 + 512],
                                 in_=sc1[:, 0:512], func=AF.Exp),
                    reads=["c1"], writes=["qha"])
                sop(lambda e: e.activation(out=sc2[:, 0:512], in_=qrw[:, :], func=AF.Exp),
                    reads=["qrw"], writes=["c2"])
                for c4 in range(4):
                    top(lambda e, c4=c4, hl=hl, tp=tp:
                        e.matmul(out=pb[4][:, 8 * hl + 4 * tp + c4:8 * hl + 4 * tp + c4 + 1],
                                 lhsT=sc2[:, 128 * c4:128 * (c4 + 1)],
                                 rhs=self.cpool[:, 8:9], start=True, stop=True),
                        reads=["c2", "ones32"], writes=["pb4"])
                vop(lambda e, t0=t0: e.tensor_tensor(out=sc3[:, 0:512], in0=qrw[:, :],
                                                     in1=self.cosf[:, HLO + t0:HLO + t0 + 512],
                                                     op=ALU.mult),
                    reads=["qrw", "cosf"], writes=["c3"])
                vop(lambda e: e.tensor_scalar(out=sc4[0:64, 0:512], in0=qrw[64:128, :],
                                              scalar1=-1.0, scalar2=None, op0=ALU.mult),
                    reads=["qrw"], writes=["c4"])
                vop(lambda e: e.tensor_copy(out=sc4[64:128, 0:512], in_=qrw[0:64, :]),
                    reads=["qrw"], writes=["c4"])
                vop(lambda e, t0=t0: e.tensor_mul(out=sc4[:, 0:512], in0=sc4[:, 0:512],
                                                  in1=self.sinf[:, HLO + t0:HLO + t0 + 512]),
                    reads=["c4", "sinf"], writes=["c4"])
                vop(lambda e, hl=hl, t0=t0:
                    e.tensor_tensor(out=self.sqh[:, T * hl + t0:T * hl + t0 + 512],
                                    in0=sc3[:, 0:512], in1=sc4[:, 0:512], op=ALU.add),
                    reads=["c3", "c4"], writes=["sqh"])
        vop(lambda e: e.reciprocal(out=self.scal[:, 0:128], in_=pb[4][:, 0:128]),
            reads=["pb4"], writes=["rzq"])
        vop(lambda e: e.tensor_scalar(out=self.scal[:, 128:256], in0=self.scal[:, 0:128],
                                      scalar1=0.5, scalar2=None, op0=ALU.mult),
            reads=["rzq"], writes=["sgla"])

    def emit_chunk_loads(self, C, hh):
        dma = self.dma
        slot = C % 2
        dma(self.ny, "y", f"dktc{slot}", self.ktc[:, 512 * slot:512 * (slot + 1)],
            self.kt_scr.rearrange("h p t -> p h t")[:, 4 * hh:4 * hh + 4, 128 * C:128 * (C + 1)],
            reads=["ktscr"], writes=[f"ktc{slot}"])
        for jj in range(2):
            j = 2 * C + jj
            for kv4 in range(4):
                dma(self.ns, "s", f"dsv{slot}",
                    self.svw[:, 1024 * slot + 128 * (4 * jj + kv4):1024 * slot + 128 * (4 * jj + kv4 + 1)],
                    self.v_scr[64 * j:64 * j + 128, 128 * (4 * hh + kv4):128 * (4 * hh + kv4 + 1)],
                    reads=["vscr"], writes=[f"svw{slot}"])
        for kv4 in range(4):
            dma(self.ns, "s", f"dskw{slot}",
                self.skw[:, 768 * slot + 192 * kv4:768 * slot + 192 * (kv4 + 1)],
                self.sk_scr[4 * hh + kv4][:, 128 * C:128 * C + 192],
                reads=["skscr"], writes=[f"skw{slot}"])

    def emit_chunk(self, C, hh):
        vop, sop, top, dma = self.vop, self.sop, self.top, self.dma
        pb, pbt = self.pb, self.pbt
        slot = C % 2
        ktk, khk, svk = f"ktc{slot}", "khc", f"svw{slot}"
        vtk = svk
        kto, svo = 512 * slot, 1024 * slot
        dma(self.ny, "y", "dkhc", self.khc[:, :],
            self.kh_scr.rearrange("h p t -> p h t")[:, 4 * hh:4 * hh + 4, 128 * C:128 * (C + 1)],
            reads=["khscr"], writes=["khc"])
        for kv4 in range(4):
            kvh = 4 * hh + kv4
            scb_ = kv4 % 2
            psk = f"pb{scb_}"
            top(lambda e, kv4=kv4: e.matmul(out=pb[scb_][:, 0:512],
                                   lhsT=self.ktc[:, kto + 128 * kv4:kto + 128 * (kv4 + 1)],
                                   rhs=self.qha.ap().rearrange("p (h n) -> p h n", h=16)
                                   [:, 4 * kv4:4 * (kv4 + 1), 128 * C:128 * (C + 1)],
                                   start=True, stop=True),
                reads=[ktk, "qha"], writes=[psk])
            vop(lambda e, kv4=kv4, kvh=kvh: e.scalar_tensor_tensor(
                    out=self.sct.ap().rearrange("p (v h n) -> p v h n", v=4, h=4)[:, kv4],
                    in0=pb[scb_].ap().rearrange("p (h n) -> p h n", h=4),
                    scalar=self.rzk[:, kvh * NCH + C:kvh * NCH + C + 1],
                    in1=self.gmask[:, :].unsqueeze(1).broadcast_to((128, 4, 128)),
                    op0=ALU.mult, op1=ALU.mult),
                reads=[psk, f"rzk{kvh}", "gmask"], writes=["sct"])
            for jj in range(2):
                j = 2 * C + jj
                sb_ = 4 if (2 * kv4 + jj) % 2 == 0 else 6
                sbk = f"pb{sb_}"
                top(lambda e, kv4=kv4, jj=jj, sb_=sb_, slot=slot:
                    e.matmul(out=pb[sb_][:, 256 * jj:256 * (jj + 1)],
                             lhsT=self.skw[:, 768 * slot + 192 * kv4 + 64 * jj:768 * slot + 192 * kv4 + 64 * jj + 128],
                             rhs=self.sqh.ap().rearrange("p (h n) -> p h n", h=16)
                             [:, 4 * kv4:4 * (kv4 + 1), 128 * C + 64 * jj:128 * C + 64 * (jj + 1)],
                             start=True, stop=True),
                    reads=[f"skw{slot}", "sqh"], writes=[sbk])
                sop(lambda e, kv4=kv4, jj=jj, sb_=sb_: e.activation(
                        out=self.ptb[:, 512 * kv4 + 256 * jj:512 * kv4 + 256 * (jj + 1)],
                        in_=pb[sb_][:, 256 * jj:256 * (jj + 1)],
                        func=AF.Exp, scale=RSQD),
                    reads=[sbk], writes=[f"ptb{kvh}"])
                mk = self.mask0 if j == 0 else self.maskg
                mkk = "mask0" if j == 0 else "maskg"
                vop(lambda e, kv4=kv4, jj=jj, mk=mk:
                    e.tensor_tensor(out=self.ptb.ap().rearrange("p (v j h n) -> p v j h n", v=4, j=2, h=4)[:, kv4, jj],
                                    in0=self.ptb.ap().rearrange("p (v j h n) -> p v j h n", v=4, j=2, h=4)[:, kv4, jj],
                                    in1=mk[:, :].unsqueeze(1).broadcast_to((128, 4, W)),
                                    op=ALU.mult),
                    reads=[f"ptb{kvh}", mkk], writes=[f"ptb{kvh}"])
        for kv4 in range(4):
            kvh = 4 * hh + kv4
            for jj in range(2):
                for h4 in range(4):
                    top(lambda e, kv4=kv4, jj=jj, h4=h4:
                        e.matmul(out=pb[5][64 * jj:64 * jj + 64, 4 * kv4 + h4:4 * kv4 + h4 + 1],
                                 lhsT=self.ptb[:, 512 * kv4 + 256 * jj + 64 * h4:512 * kv4 + 256 * jj + 64 * (h4 + 1)],
                                 rhs=self.onescol[:, 0:1], start=True, stop=True),
                        reads=[f"ptb{kvh}", "onesb"], writes=["pb5"])
        vop(lambda e: e.reciprocal(out=self.sswa[:, 0:16], in_=pb[5][:, 0:16]),
            reads=["pb5"], writes=["rzs"])
        vop(lambda e: e.tensor_scalar(out=self.sswa[:, 16:32], in0=self.sswa[:, 0:16],
                                      scalar1=0.5, scalar2=None, op0=ALU.mult),
            reads=["rzs"], writes=["rzs2"])
        for kv4 in range(4):
            kvh = 4 * hh + kv4
            ob = 2 + kv4 % 2
            pvb = 5 + kv4 % 2
            pok, ppk = f"pb{ob}", f"pb{pvb}"
            for h4 in range(4):
                hl = 4 * kv4 + h4
                top(lambda e, kv4=kv4, h4=h4:
                    e.matmul(out=pb[ob][:, 128 * h4:128 * (h4 + 1)],
                             lhsT=self.sct[:, 512 * kv4 + 128 * h4:512 * kv4 + 128 * (h4 + 1)],
                             rhs=self.svw[:, svo + 128 * (4 + kv4):svo + 128 * (5 + kv4)],
                             start=True, stop=False),
                    reads=["sct", vtk], writes=[pok])
                top(lambda e, kvh=kvh, h4=h4, hl=hl:
                    e.matmul(out=pb[ob][:, 128 * h4:128 * (h4 + 1)],
                             lhsT=self.qha[:, T * hl + 128 * C:T * hl + 128 * (C + 1)],
                             rhs=self.sbf[:, 128 * kvh:128 * (kvh + 1)],
                             start=False, stop=True),
                    reads=["qha", "sbf"], writes=[pok])
                for jj in range(2):
                    top(lambda e, kv4=kv4, h4=h4, jj=jj:
                        e.matmul(out=pb[pvb][64 * jj:64 * jj + 64, 128 * h4:128 * (h4 + 1)],
                                 lhsT=self.ptb[:, 512 * kv4 + 256 * jj + 64 * h4:512 * kv4 + 256 * jj + 64 * (h4 + 1)],
                                 rhs=self.svw[:, svo + 128 * (4 * jj + kv4):svo + 128 * (4 * jj + kv4 + 1)],
                                 start=True, stop=True),
                        reads=[f"ptb{kvh}", svk], writes=[ppk])
            top(lambda e, kv4=kv4, kvh=kvh: e.matmul(out=pb[4][:, 128 * kv4:128 * kv4 + 128],
                                   lhsT=self.khc[:, 128 * kv4:128 * (kv4 + 1)],
                                   rhs=self.svw[:, svo + 128 * (4 + kv4):svo + 128 * (5 + kv4)],
                                   start=True, stop=True),
                reads=[khk, vtk], writes=["pb4"])
            vop(lambda e, kv4=kv4: e.tensor_tensor(
                    out=self.qrw.ap().rearrange("p (h n) -> p h n", h=4),
                    in0=pb[ob].ap().rearrange("p (h n) -> p h n", h=4),
                    in1=self.scal.ap().rearrange("p (o h c) -> p o h c", o=2, c=NCH)
                    [:, 1, 4 * kv4:4 * (kv4 + 1), C:C + 1].broadcast_to((128, 4, 128)),
                    op=ALU.mult),
                reads=[pok, "sgla"], writes=["qrw"])
            vop(lambda e, kv4=kv4: e.tensor_tensor(
                    out=self.ahq[:, 0:512].rearrange("p (h n) -> p h n", h=4),
                    in0=pb[pvb].ap().rearrange("p (h n) -> p h n", h=4),
                    in1=self.sswa.rearrange("p (o h) -> p o h", o=2)
                    [:, 1, 4 * kv4:4 * (kv4 + 1)].unsqueeze(2).broadcast_to((128, 4, 128)),
                    op=ALU.mult),
                reads=[ppk, "rzs2"], writes=["ahq"])
            oslot = kv4 % 2
            vop(lambda e, oslot=oslot: e.tensor_tensor(out=self.omix[:, 512 * oslot:512 * (oslot + 1)],
                                          in0=self.qrw[:, :], in1=self.ahq[:, 0:512], op=ALU.add),
                reads=["qrw", "ahq"], writes=[f"omix{oslot}"])
            dma(self.ny, "y", "dom",
                self.attn_scr[128 * C:128 * (C + 1), 2048 * hh + 512 * kv4:2048 * hh + 512 * (kv4 + 1)],
                self.omix[:, 512 * oslot:512 * (oslot + 1)],
                reads=[f"omix{oslot}"], writes=["ascr2"])
            vop(lambda e, kvh=kvh, kv4=kv4: e.tensor_add(out=self.s32[:, 128 * kvh:128 * (kvh + 1)],
                                       in0=self.s32[:, 128 * kvh:128 * (kvh + 1)],
                                       in1=pb[4][:, 128 * kv4:128 * kv4 + 128]),
                reads=["pb4", "s32"], writes=["s32"])
            sop(lambda e, kvh=kvh: e.activation(out=self.sbf[:, 128 * kvh:128 * (kvh + 1)],
                                       in_=self.s32[:, 128 * kvh:128 * (kvh + 1)], func=AF.Copy),
                reads=["s32"], writes=["sbf"])


def _fixup_emit(self):
    pass


def _host_tables(p0):
    pos = np.arange(p0 - HLO, p0 + T, dtype=np.float64)
    half = np.arange(64, dtype=np.float64)
    inv = 1.0 / (10000.0 ** (2.0 * half / 128.0))
    dvec = inv[np.concatenate([np.arange(64), np.arange(64)])]
    ang = dvec[:, None] * pos[None, :]
    return np.cos(ang).astype(np.float32), np.sin(ang).astype(np.float32)


def _host_masks(first_half):
    a = np.arange(W)[None, :]
    c = np.arange(2 * W)[:, None]
    g = ((a <= c) & (c <= a + W)).astype(np.float32)
    m0 = g * (c >= W) if first_half else g
    return np.ascontiguousarray(m0.astype(np.float32)), g


_CACHE = {}


def kernel(L0_attn_hidden_states, L0_attn_Wq, L0_attn_Wk, L0_attn_Wv, L0_attn_Wo,
           L0_ffn_weight_q, L0_ffn_weight_scale, L0_ffn_bias):
    X = np.ascontiguousarray(L0_attn_hidden_states, dtype=np.float32)
    def pack(w, nt):
        n = w.shape[1] // nt
        return np.ascontiguousarray(
            w.reshape(32, 128, nt, n).transpose(2, 1, 0, 3))
    wk = pack(np.asarray(L0_attn_Wk, np.float32), 8)
    wv = pack(np.asarray(L0_attn_Wv, np.float32), 2)
    wq = pack(np.asarray(L0_attn_Wq, np.float32), 32)
    wo = pack(np.asarray(L0_attn_Wo, np.float32), 8)
    wf = pack(np.ascontiguousarray(L0_ffn_weight_q.T), 8)
    wsb = np.ascontiguousarray(np.broadcast_to(L0_ffn_weight_scale[None, :], (128, HID)), dtype=np.float32)
    bb = np.ascontiguousarray(np.broadcast_to(L0_ffn_bias[None, :], (128, HID)), dtype=np.float32)
    gm = np.tril(np.ones((128, 128), np.float32)).T
    gm = np.ascontiguousarray(gm)
    ident = np.eye(128).astype(BF)

    if "nc" not in _CACHE:
        kb = KB(bass.Bass("TRN2", target_bir_lowering=False))
        _CACHE["nc"] = kb.build()
    nc = _CACHE["nc"]

    in_maps = []
    zh = np.zeros((T, HID), np.float32)
    zhalo = np.zeros((HLO, HID), np.float32)
    for c in range(NCORES):
        b, h = c // 2, c % 2
        p0 = h * T
        halo = X[b, p0 - HLO:p0] if h == 1 else zhalo
        x = np.ascontiguousarray(np.concatenate([halo, X[b, p0:p0 + T]], axis=0))
        xh = np.ascontiguousarray(X[b, 0:T]) if h == 1 else zh
        cosf, sinf = _host_tables(p0)
        m0, mg = _host_masks(h == 0)
        in_maps.append({
            "x": x, "xh": xh, "wk": wk, "wv": wv, "wq": wq, "wo": wo, "wf": wf,
            "wsb": wsb, "bb": bb,
            "cosf": np.ascontiguousarray(cosf), "sinf": np.ascontiguousarray(sinf),
            "mask0": m0, "maskg": np.ascontiguousarray(mg), "gmask": gm, "ident": ident,
        })
    res = run_bass_kernel_spmd(nc, in_maps, core_ids=list(range(NCORES)))
    out = np.empty((4, 2048, HID), np.float32)
    for c in range(NCORES):
        b, h = c // 2, c % 2
        out[b, h * T:(h + 1) * T] = res.results[c]["out"]
    return out
